# version 18
# baseline (speedup 1.0000x reference)
import numpy as np
import concourse.bass as bass
import concourse.mybir as mybir
from concourse.bass_utils import run_bass_kernel_spmd
from contextlib import ExitStack

F32 = mybir.dt.float32
BF16 = mybir.dt.bfloat16
AF = mybir.ActivationFunctionType
ALU = mybir.AluOpType
AX = mybir.AxisListType


class Buf:
    __slots__ = ("wr", "readers", "name")

    def __init__(self, name=""):
        self.wr = None
        self.readers = []
        self.name = name


class Chan:
    __slots__ = ("name", "count", "sem", "inflight_from")

    def __init__(self, name):
        self.name = name
        self.count = 0
        self.sem = None
        self.inflight_from = None


class Stream:
    def __init__(self, name, engine_attr, dma=False):
        self.name = name
        self.engine_attr = engine_attr
        self.dma = dma
        self.ops = []
        self.watermark = {}
        self.chan = Chan(name)


class Sched:
    def __init__(self):
        self.streams = {}
        self.chans = []

    def stream(self, name, engine_attr, dma=False):
        s = Stream(name, engine_attr, dma)
        self.streams[name] = s
        if s.chan is not None:
            self.chans.append(s.chan)
        return s

    def chan(self, name):
        c = Chan(name)
        self.chans.append(c)
        return c

    def op(self, stream, fn, reads=(), writes=(), chan=None, strict=True):
        dma = chan is not None
        if dma:
            step = 16
        else:
            chan = stream.chan
            step = 1
        waits = {}

        def need(mark):
            if mark is None:
                return
            c, val = mark
            if waits.get(c, 0) < val:
                waits[c] = val

        for b in reads:
            need(b.wr)
        for b in writes:
            need(b.wr)
            for r in b.readers:
                need(r)
        if dma and strict and chan.inflight_from is not None:
            need((chan, chan.inflight_from + 16))
        final = {}
        for c, val in waits.items():
            if stream.watermark.get(c, 0) < val:
                final[c] = val
                stream.watermark[c] = val
        if dma:
            chan.inflight_from = chan.count
        chan.count += step
        mark = (chan, chan.count)
        for b in reads:
            b.readers.append(mark)
        for b in writes:
            b.wr = mark
            b.readers = []
        stream.ops.append((final, fn, chan, step))
        return mark

    def wait(self, stream, marks):
        final = {}
        for c, val in marks:
            if stream.watermark.get(c, 0) < val:
                final[c] = val
                stream.watermark[c] = val
        if final:
            stream.ops.append((final, None, None, 0))

    def barrier(self):
        marks = [(c, c.count) for c in self.chans if c.count]
        for s in self.streams.values():
            self.wait(s, marks)

    def build(self, nc, block, es):
        for c in self.chans:
            c.sem = es.enter_context(nc.semaphore(f"sem_{c.name}"))

        def make_body(s):
            def body(eng):
                for waits, fn, chan, step in s.ops:
                    for c, val in waits.items():
                        eng.wait_ge(c.sem, val)
                    if fn is None:
                        continue
                    instr = fn(eng)
                    if instr is None:
                        raise RuntimeError(f"{s.name}: fn returned no instruction")
                    instr.then_inc(chan.sem, step)
            return body

        for s in self.streams.values():
            if not s.ops:
                continue
            getattr(block, s.engine_attr)(make_body(s))


class Cfg:
    def __init__(self, B=2, L=4096, D=1024, DIN=2048, N=64, R=64, E=8,
                 HS=2048, HR=1024, T1=512, T2=512, T2B=512, T3=512, T3B=512,
                 GN=8, DHALF=8):
        self.B, self.L, self.D, self.DIN = B, L, D, DIN
        self.N, self.R, self.E, self.HS, self.HR = N, R, E, HS, HR
        self.T1, self.T2, self.T2B, self.T3, self.T3B = T1, T2, T2B, T3, T3B
        self.GN = GN
        self.DHALF = DHALF
        self.CONV = 4
        self.NT = B * L
        self.DKC = D // 128
        self.DINKC = DIN // 128
        self.XCOLS = 2 * DIN
        self.PCOLS = R + 2 * N
        assert L % T1 == 0 and L % T2 == 0 and self.NT % T2B == 0
        assert self.NT % T3 == 0 and self.NT % T3B == 0
        assert N % GN == 0 and self.DINKC % DHALF == 0


FULL = Cfg()


def build_core(nc, cfg):
    c = cfg
    P = 128
    SEG = c.T2 + 1
    NG = c.N // c.GN

    def din(name, shape, dt=F32):
        return nc.dram_tensor(name, list(shape), dt, kind="ExternalInput").ap()

    x_in = din("x", (c.B, c.L, c.D))
    w_in = din("W_in", (c.D, c.XCOLS))
    conv_w = din("conv_w_pt", (P, c.DINKC * c.CONV))
    conv_b = din("conv_b_pt", (P, c.DINKC))
    w_xp = din("W_xproj", (c.DIN, c.PCOLS))
    w_dt = din("W_dt", (c.R, c.DIN))
    b_dt = din("b_dt_pt", (P, c.DINKC))
    a_log = din("A_log_pt", (P, c.DINKC * c.N))
    d_vec = din("D_pt", (P, c.DINKC))
    w_out = din("W_out", (c.DIN, c.D))
    ln_w = din("ln_w_pt", (P, c.DKC))
    ln_b = din("ln_b_pt", (P, c.DKC))
    rw = din("router_w_pt", (P, c.DKC * c.E))
    rbias = din("router_bias_bc", (P, c.E))
    sw1 = din("sw1", (c.D, c.HS))
    sw3 = din("sw3", (c.D, c.HS))
    sw2 = din("sw2", (c.HS, c.D))
    ew1 = din("ew1", (c.E, c.D, c.HR))
    ew2 = din("ew2", (c.E, c.HR, c.D))
    ident_in = din("ident_bf", (P, P), BF16)
    seln_in = din("sel_n", (c.N, c.N * P), BF16)
    sele_in = din("sel_e", (c.E, c.E * P), BF16)

    out = nc.dram_tensor("out", [c.B, c.L, c.D], F32, kind="ExternalOutput").ap()

    def scr(name, shape, dt=BF16):
        return nc.dram_tensor(name, list(shape), dt, kind="Internal").ap()

    sc_u = scr("sc_u", (c.DIN, c.NT))
    sc_z = scr("sc_z", (c.DIN, c.NT))
    sc_d = scr("sc_d", (c.DIN, c.NT))
    sc_B = scr("sc_B", (c.N, c.NT))
    sc_C = scr("sc_C", (c.N, c.NT))
    sc_yg = scr("sc_yg", (c.DIN, c.NT))
    sc_x1 = scr("sc_x1", (c.NT, c.D), F32)
    sc_xn3 = scr("sc_xn3", (c.D, c.NT))
    sc_sh = scr("sc_sh", (c.D, c.NT))
    sc_comb = scr("sc_comb", (c.E, c.NT))
    sc_wout = scr("sc_wout", (c.DIN, c.D))
    sc_sw1 = scr("sc_sw1", (c.D, c.HS))
    sc_sw3 = scr("sc_sw3", (c.D, c.HS))
    sc_sw2 = scr("sc_sw2", (c.HS, c.D))
    sc_ew1 = scr("sc_ew1", (c.E, c.D, c.HR))
    sc_ew2 = scr("sc_ew2", (c.E, c.HR, c.D))

    xv = x_in.rearrange("b (tl p) d -> b p tl d", p=P)
    ov = out.rearrange("b (tl p) d -> b p tl d", p=P)
    uv = sc_u.rearrange("(kc p) t -> p kc t", p=P)
    zv = sc_z.rearrange("(kc p) t -> p kc t", p=P)
    dv = sc_d.rearrange("(kc p) t -> p kc t", p=P)
    ygv = sc_yg.rearrange("(kc p) t -> p kc t", p=P)
    x1v = sc_x1.rearrange("(tl p) d -> p tl d", p=P)
    xn3v = sc_xn3.rearrange("(kc p) t -> p kc t", p=P)
    shv = sc_sh.rearrange("(kc p) t -> p kc t", p=P)
    woutv = sc_wout.rearrange("(kc p) d -> p kc d", p=P)
    sw1v = sc_sw1.rearrange("(kc p) h -> p kc h", p=P)
    sw3v = sc_sw3.rearrange("(kc p) h -> p kc h", p=P)
    sw2v = sc_sw2.rearrange("(kc p) d -> p kc d", p=P)
    ew1v = sc_ew1.rearrange("e (kc p) h -> e p kc h", p=P)
    ew2v = sc_ew2.rearrange("e (kc p) d -> e p kc d", p=P)

    K = Sched()
    sp = K.stream("sp", "sync", dma=True)
    pool = K.stream("pool", "gpsimd")
    dve = K.stream("dve", "vector")
    act = K.stream("act", "scalar")
    pe = K.stream("pe", "tensor")

    CH = {}
    def ch(name):
        if name not in CH:
            CH[name] = K.chan(name)
        return CH[name]

    bufs = {}
    def bf(name):
        if name not in bufs:
            bufs[name] = Buf(name)
        return bufs[name]

    es_g = ExitStack()

    def sbuf(es, name, shape, dt=F32):
        return es.enter_context(nc.sbuf_tensor(name, list(shape), dt))

    ident = sbuf(es_g, "ident", (P, P), BF16)
    sele = sbuf(es_g, "sele", (c.E, c.E * P), BF16)
    biasbc = sbuf(es_g, "biasbc", (P, c.E))
    cw_c = sbuf(es_g, "cw_c", (P, c.DINKC * c.CONV))
    cb_c = sbuf(es_g, "cb_c", (P, c.DINKC))
    bdt_c = sbuf(es_g, "bdt_c", (P, c.DINKC))
    dv_c = sbuf(es_g, "dv_c", (P, c.DINKC))
    lnw_c = sbuf(es_g, "lnw_c", (P, c.DKC))
    lnb_c = sbuf(es_g, "lnb_c", (P, c.DKC))
    rw_bf = sbuf(es_g, "rw_bf", (P, c.DKC * c.E), BF16)
    a_sb = sbuf(es_g, "a_sb", (P, c.DINKC * c.N))
    ntl = c.NT // 128
    mu_all = sbuf(es_g, "mu_all", (P, ntl))
    r1_all = sbuf(es_g, "r1_all", (P, ntl))
    muc_all = sbuf(es_g, "muc_all", (P, ntl))
    rc_all = sbuf(es_g, "rc_all", (P, ntl))
    h_st = sbuf(es_g, "h_st", (P, c.DINKC * c.N))
    eps_col = sbuf(es_g, "eps_col", (P, 1))
    half_col = sbuf(es_g, "half_col", (P, 1))
    colA = sbuf(es_g, "colA", (P, 1))
    colB = sbuf(es_g, "colB", (P, 1))
    colC = sbuf(es_g, "colC", (P, 1))
    colD = sbuf(es_g, "colD", (P, 1))
    colE = sbuf(es_g, "colE", (P, 1))
    colF = sbuf(es_g, "colF", (P, 1))
    colG = sbuf(es_g, "colG", (P, 1))
    sqjunk = sbuf(es_g, "sqjunk", (P, max(c.D, 1024)), BF16)

    pm = [es_g.enter_context(nc.psum_tensor(f"pm{i}", [P, 512], F32)) for i in range(4)]
    ptr = [es_g.enter_context(nc.psum_tensor(f"ptr{i}", [P, P], BF16)) for i in range(2)]
    psm = es_g.enter_context(nc.psum_tensor("psm", [P, 64], F32))
    py = [es_g.enter_context(nc.psum_tensor("py0", [P, 512], F32))] * 2

    def ld(dst_ap, src_ap, chan_name, wbuf, rb=()):
        K.op(sp, lambda e, d=dst_ap, s=src_ap: e.dma_start(out=d, in_=s),
             reads=rb, writes=(wbuf,), chan=ch(chan_name))

    def st(dst_ap, src_ap, chan_name, rbuf, wb=()):
        K.op(pool, lambda e, d=dst_ap, s=src_ap: e.dma_start(out=d, in_=s),
             reads=(rbuf,), writes=wb, chan=ch(chan_name))

    def vop(fn, reads=(), writes=()):
        K.op(dve, fn, reads=reads, writes=writes)

    def aop(fn, reads=(), writes=()):
        K.op(act, fn, reads=reads, writes=writes)

    def mmacc(psum_ap, pairs, reads, wbuf):
        def fn(e):
            last = None
            n = len(pairs)
            for i, (s, m) in enumerate(pairs):
                last = e.matmul(psum_ap, s, m, start=(i == 0), stop=(i == n - 1))
            return last
        K.op(pe, fn, reads=reads, writes=(wbuf,))

    def transp(out_ap, in_ap, reads, wbuf):
        K.op(pe, lambda e: e.transpose(out_ap, in_ap, ident[:]),
             reads=tuple(reads) + (bf("ident"),), writes=(wbuf,))


    vop(lambda e: e.memset(eps_col[:], 1e-5), writes=(bf("eps"),))
    vop(lambda e: e.memset(half_col[:], 0.5), writes=(bf("half"),))
    ld(ident[:], ident_in, "ld_c", bf("ident"))
    ld(sele[:], sele_in, "ld_c", bf("sele"))
    ld(biasbc[:], rbias, "ld_c", bf("biasbc"))
    ld(cw_c[:], conv_w, "ld_c", bf("cw"))
    ld(cb_c[:], conv_b, "ld_c", bf("cb"))
    ld(bdt_c[:], b_dt, "ld_c", bf("bdt"))
    ld(dv_c[:], d_vec, "ld_c", bf("dv"))
    ld(lnw_c[:], ln_w, "ld_c", bf("lnw"))

    es_p1 = ExitStack()
    w_in_sb = sbuf(es_p1, "w_in_sb", (P, c.DKC, c.XCOLS), BF16)
    w_xp_sb = sbuf(es_p1, "w_xp_sb", (P, c.DINKC, c.PCOLS), BF16)
    w_dt_sb = sbuf(es_p1, "w_dt_sb", (c.R, c.DIN), BF16)

    es_i = ExitStack()
    stage = sbuf(es_i, "stage", (P, 4096))

    ld(lnb_c[:], ln_b, "ld_c", bf("lnb"))
    ld(stage[:, : c.DKC * c.E], rw, "ld_c", bf("stage"))
    vop(lambda e: e.tensor_copy(rw_bf[:], stage[:, : c.DKC * c.E]),
        reads=(bf("stage"),), writes=(bf("rw"),))
    ld(stage[:, : c.DINKC * c.N], a_log, "ld_c", bf("stage"), rb=(bf("rw"),))
    aop(lambda e: e.activation(out=a_sb[:], in_=stage[:, : c.DINKC * c.N],
                               func=AF.Exp),
        reads=(bf("stage"),), writes=(bf("a_sb"),))
    vop(lambda e: e.tensor_scalar(out=a_sb[:], in0=a_sb[:], scalar1=-1.0,
                                  scalar2=None, op0=ALU.mult),
        reads=(bf("a_sb"),), writes=(bf("a_sb"),))

    w_in_v = w_in.rearrange("(kc p) x -> p kc x", p=P)
    for kc in range(c.DKC):
        ld(stage[:, : c.XCOLS], w_in_v[:, kc, :], "ld_w", bf("stage"),
           rb=(bf("w_in_sb"),) if kc else (bf("a_sb"),))
        vop(lambda e, kc=kc: e.tensor_copy(w_in_sb[:, kc, :],
                                           stage[:, : c.XCOLS]),
            reads=(bf("stage"),), writes=(bf("w_in_sb"),))
    w_xp_v = w_xp.rearrange("(kc p) x -> p kc x", p=P)
    for kc in range(c.DINKC):
        ld(stage[:, : c.PCOLS], w_xp_v[:, kc, :], "ld_w", bf("stage"),
           rb=(bf("w_xp_sb"),) if kc else (bf("w_in_sb"),))
        vop(lambda e, kc=kc: e.tensor_copy(w_xp_sb[:, kc, :], stage[:, : c.PCOLS]),
            reads=(bf("stage"),), writes=(bf("w_xp_sb"),))
    ld(stage[: c.R, : c.DIN], w_dt, "ld_w", bf("stage"), rb=(bf("w_xp_sb"),))
    vop(lambda e: e.tensor_copy(w_dt_sb[:], stage[: c.R, : c.DIN]),
        reads=(bf("stage"),), writes=(bf("w_dt_sb"),))


    NSTG = 4
    stages = [sbuf(es_i, f"stage{i}", (P, 4096)) for i in range(NSTG)]
    stbfs = [sbuf(es_i, f"stbf{i}", (P, 4096), BF16) for i in range(NSTG)]
    _wc = [0]

    def wcast(src, dst, rows, cols, name):
        nch = rows // P
        for i in range(nch):
            par = _wc[0] % NSTG
            _wc[0] += 1
            stg, stb = stages[par], stbfs[par]
            sgb = bf(f"stageb{par}")
            sbb = bf(f"stbfb{par}")
            K.op(sp, lambda e, s=src, i=i, stg=stg, cols=cols: e.dma_start(
                out=stg[:, :cols], in_=s[i * P : (i + 1) * P, :]),
                reads=(sbb,), writes=(sgb,), chan=ch(f"ld_w{par}"))
            aop(lambda e, stg=stg, stb=stb, cols=cols: e.activation(
                out=stb[:, :cols], in_=stg[:, :cols], func=AF.Copy),
                reads=(sgb,), writes=(sbb,))
            K.op(pool, lambda e, d=dst, i=i, stb=stb, cols=cols: e.dma_start(
                out=d[i * P : (i + 1) * P, :], in_=stb[:, :cols]),
                reads=(sbb,), writes=(), chan=ch(f"st_w{par}"))

    wcast(w_out, sc_wout, c.DIN, c.D, "wout")
    wcast(sw1, sc_sw1, c.D, c.HS, "sw1")
    wcast(sw3, sc_sw3, c.D, c.HS, "sw3")
    wcast(sw2, sc_sw2, c.HS, c.D, "sw2")
    for e_ in range(c.E):
        wcast(ew1[e_], sc_ew1[e_], c.D, c.HR, f"ew1_{e_}")
        wcast(ew2[e_], sc_ew2[e_], c.HR, c.D, f"ew2_{e_}")

    K.barrier()
    es_i.close()

    es_p1b = ExitStack()
    ntt1 = c.T1 // 128
    x_sb = sbuf(es_p1b, "x_sb", (P, ntt1, c.D))
    xn2_tm = sbuf(es_p1b, "xn2_tm", (P, ntt1, c.D), BF16)
    xn2T = sbuf(es_p1b, "xn2T", (P, c.DKC, c.T1), BF16)
    xi_sb = sbuf(es_p1b, "xi_sb", (P, c.DINKC, c.CONV - 1 + c.T1), BF16)
    z_sb = sbuf(es_p1b, "z_sb", (P, c.DINKC, c.T1), BF16)
    u_sb = sbuf(es_p1b, "u_sb", (P, c.DINKC, c.T1), BF16)
    caccs = [sbuf(es_p1b, f"cacc{i}", (P, c.T1)) for i in range(2)]
    t_sp1s = [sbuf(es_p1b, f"t_sp1_{i}", (P, c.T1)) for i in range(2)]
    t_sp2s = [sbuf(es_p1b, f"t_sp2_{i}", (P, c.T1)) for i in range(2)]
    d_sb = sbuf(es_p1b, "d_sb", (P, c.DINKC, c.T1), BF16)
    dtr_bf = sbuf(es_p1b, "dtr_bf", (c.R, c.T1), BF16)
    B_bf = sbuf(es_p1b, "B_bf", (c.N, c.T1), BF16)
    C_bf = sbuf(es_p1b, "C_bf", (c.N, c.T1), BF16)

    nch1 = c.L // c.T1
    for b in range(c.B):
        for cc1 in range(nch1):
            tok0 = b * c.L + cc1 * c.T1
            tl0 = cc1 * ntt1
            ld(x_sb[:], xv[b, :, tl0 : tl0 + ntt1, :], "ld_x", bf("x_sb"))
            for tt in range(ntt1):
                idx = tok0 // 128 + tt
                x_t = x_sb[:, tt, :]
                vop(lambda e, x_t=x_t: e.reduce_sum(out=colA[:], in_=x_t, axis=AX.X),
                    reads=(bf("x_sb"),), writes=(bf("colA"),))
                aop(lambda e, x_t=x_t: e.activation(out=sqjunk[:, : c.D], in_=x_t,
                                                    func=AF.Square, accum_out=colB[:]),
                    reads=(bf("x_sb"),), writes=(bf("colB"), bf("sqjunk")))
                vop(lambda e, idx=idx: e.tensor_scalar(
                    out=mu_all[:, idx : idx + 1], in0=colA[:], scalar1=1.0 / c.D,
                    scalar2=None, op0=ALU.mult),
                    reads=(bf("colA"),), writes=(bf("mu_all"),))
                vop(lambda e: e.tensor_scalar(out=colC[:], in0=colB[:],
                                              scalar1=1.0 / c.D, scalar2=None,
                                              op0=ALU.mult),
                    reads=(bf("colB"),), writes=(bf("colC"),))
                vop(lambda e, idx=idx: e.scalar_tensor_tensor(
                    out=colD[:], in0=mu_all[:, idx : idx + 1],
                    scalar=mu_all[:, idx : idx + 1], in1=colC[:],
                    op0=ALU.mult, op1=ALU.subtract),
                    reads=(bf("mu_all"), bf("colC")), writes=(bf("colD"),))
                vop(lambda e: e.tensor_scalar(out=colD[:], in0=colD[:], scalar1=-1.0,
                                              scalar2=None, op0=ALU.mult),
                    reads=(bf("colD"),), writes=(bf("colD"),))
                aop(lambda e: e.activation(out=colC[:], in_=colD[:], func=AF.Sqrt,
                                           bias=eps_col[:]),
                    reads=(bf("colD"), bf("eps")), writes=(bf("colC"),))
                vop(lambda e, idx=idx: e.reciprocal(out=r1_all[:, idx : idx + 1],
                                                    in_=colC[:]),
                    reads=(bf("colC"),), writes=(bf("r1_all"),))
                vop(lambda e, idx=idx: e.tensor_tensor(
                    out=colE[:], in0=r1_all[:, idx : idx + 1],
                    in1=r1_all[:, idx : idx + 1], op=ALU.mult),
                    reads=(bf("r1_all"),), writes=(bf("colE"),))
                vop(lambda e: e.tensor_tensor(out=colE[:], in0=colD[:], in1=colE[:],
                                              op=ALU.mult),
                    reads=(bf("colD"), bf("colE")), writes=(bf("colE"),))
                aop(lambda e: e.activation(out=colE[:], in_=colE[:], func=AF.Sqrt,
                                           bias=eps_col[:]),
                    reads=(bf("colE"), bf("eps")), writes=(bf("colE"),))
                vop(lambda e: e.reciprocal(out=colE[:], in_=colE[:]),
                    reads=(bf("colE"),), writes=(bf("colE"),))
                vop(lambda e, idx=idx: e.tensor_tensor(
                    out=colE[:], in0=colE[:], in1=r1_all[:, idx : idx + 1],
                    op=ALU.mult),
                    reads=(bf("colE"), bf("r1_all")), writes=(bf("colE"),))
                vop(lambda e, x_t=x_t, tt=tt, idx=idx: e.tensor_scalar(
                    out=xn2_tm[:, tt, :], in0=x_t,
                    scalar1=mu_all[:, idx : idx + 1], scalar2=colE[:],
                    op0=ALU.subtract, op1=ALU.mult),
                    reads=(bf("x_sb"), bf("mu_all"), bf("colE")),
                    writes=(bf("xn2_tm"),))
            for tt in range(ntt1):
                for dkc in range(c.DKC):
                    pt = ptr[(tt * c.DKC + dkc) % 2]
                    pbuf = bf(f"ptr{(tt * c.DKC + dkc) % 2}")
                    transp(pt[:], xn2_tm[:, tt, dkc * 128 : (dkc + 1) * 128],
                           (bf("xn2_tm"),), pbuf)
                    vop(lambda e, pt=pt, dkc=dkc, tt=tt: e.tensor_scalar(
                        out=xn2T[:, dkc, tt * 128 : (tt + 1) * 128], in0=pt[:],
                        scalar1=lnw_c[:, dkc : dkc + 1],
                        scalar2=lnb_c[:, dkc : dkc + 1],
                        op0=ALU.mult, op1=ALU.add),
                        reads=(pbuf, bf("lnw"), bf("lnb")), writes=(bf("xn2T"),))
            for dc in range(c.DINKC):
                if cc1 == 0:
                    vop(lambda e, dc=dc: e.memset(xi_sb[:, dc, 0 : c.CONV - 1], 0.0),
                        reads=(bf("xi"),), writes=(bf("xi"),))
                else:
                    vop(lambda e, dc=dc: e.tensor_copy(
                        xi_sb[:, dc, 0 : c.CONV - 1],
                        xi_sb[:, dc, c.T1 : c.T1 + c.CONV - 1]),
                        reads=(bf("xi"),), writes=(bf("xi"),))
            for cc in range(c.XCOLS // 128):
                p = pm[cc % 2]
                pb = bf(f"pm{cc % 2}")
                pairs = [(w_in_sb[:, kc, cc * 128 : (cc + 1) * 128], xn2T[:, kc, :])
                         for kc in range(c.DKC)]
                mmacc(p[:, : c.T1], pairs, (bf("w_in_sb"), bf("xn2T")), pb)
                if cc < c.DINKC:
                    dst = xi_sb[:, cc, c.CONV - 1 : c.CONV - 1 + c.T1]
                    wb = bf("xi")
                else:
                    dst = z_sb[:, cc - c.DINKC, :]
                    wb = bf("z_sb")
                vop(lambda e, p=p, dst=dst: e.tensor_copy(dst, p[:, : c.T1]),
                    reads=(pb,), writes=(wb,))
            for dc in range(c.DINKC):
                cacc = caccs[dc % 2]
                cb_ = bf(f"cacc{dc % 2}")
                vop(lambda e, dc=dc, cacc=cacc: e.tensor_scalar(
                    out=cacc[:], in0=xi_sb[:, dc, 0 : c.T1],
                    scalar1=cw_c[:, dc * c.CONV : dc * c.CONV + 1], scalar2=None,
                    op0=ALU.mult),
                    reads=(bf("xi"), bf("cw")), writes=(cb_,))
                for k in range(1, c.CONV):
                    vop(lambda e, dc=dc, k=k, cacc=cacc: e.scalar_tensor_tensor(
                        out=cacc[:], in0=xi_sb[:, dc, k : k + c.T1],
                        scalar=cw_c[:, dc * c.CONV + k : dc * c.CONV + k + 1],
                        in1=cacc[:], op0=ALU.mult, op1=ALU.add),
                        reads=(bf("xi"), bf("cw"), cb_), writes=(cb_,))
                aop(lambda e, dc=dc, cacc=cacc: e.activation(
                    out=u_sb[:, dc, :], in_=cacc[:], func=AF.Silu,
                    bias=cb_c[:, dc : dc + 1]),
                    reads=(cb_, bf("cb")), writes=(bf("u_sb"),))
            st(uv[:, :, tok0 : tok0 + c.T1], u_sb[:], "st_u", bf("u_sb"), wb=(bf("sc_u"),))
            st(zv[:, :, tok0 : tok0 + c.T1], z_sb[:], "st_z", bf("z_sb"), wb=(bf("sc_z"),))
            nparts = (c.PCOLS + 127) // 128
            for part in range(nparts):
                cw_ = min(128, c.PCOLS - part * 128)
                p = pm[2 + part % 2]
                pb = bf(f"pm{2 + part % 2}")
                pairs = [(w_xp_sb[:, kc, part * 128 : part * 128 + cw_],
                          u_sb[:, kc, :]) for kc in range(c.DINKC)]
                mmacc(p[:cw_, : c.T1], pairs, (bf("w_xp_sb"), bf("u_sb")), pb)
                for (name, tile, wb, lo, hi) in (
                    ("dtr", dtr_bf, bf("dtr"), 0, c.R),
                    ("B", B_bf, bf("B_bf"), c.R, c.R + c.N),
                    ("C", C_bf, bf("C_bf"), c.R + c.N, c.R + 2 * c.N),
                ):
                    s0 = max(lo, part * 128)
                    s1 = min(hi, part * 128 + cw_)
                    if s0 >= s1:
                        continue
                    vop(lambda e, p=p, s0=s0, s1=s1, part=part, tile=tile, lo=lo:
                        e.tensor_copy(tile[s0 - lo : s1 - lo, :],
                                      p[s0 - part * 128 : s1 - part * 128, : c.T1]),
                        reads=(pb,), writes=(wb,))
            st(sc_B[:, tok0 : tok0 + c.T1], B_bf[:], "st_B", bf("B_bf"), wb=(bf("sc_B"),))
            st(sc_C[:, tok0 : tok0 + c.T1], C_bf[:], "st_C", bf("C_bf"), wb=(bf("sc_C"),))
            for dc in range(c.DINKC):
                p = pm[dc % 2]
                pb = bf(f"pm{dc % 2}")
                t_sp1, t_sp2 = t_sp1s[dc % 2], t_sp2s[dc % 2]
                tb1, tb2 = bf(f"t_sp1_{dc % 2}"), bf(f"t_sp2_{dc % 2}")
                pairs = [(w_dt_sb[:, dc * 128 : (dc + 1) * 128], dtr_bf[:])]
                mmacc(p[:, : c.T1], pairs, (bf("w_dt_sb"), bf("dtr")), pb)
                vop(lambda e, p=p, dc=dc, t_sp2=t_sp2: e.tensor_scalar(
                    out=t_sp2[:], in0=p[:, : c.T1],
                    scalar1=bdt_c[:, dc : dc + 1], scalar2=None, op0=ALU.add),
                    reads=(pb, bf("bdt")), writes=(tb2,))
                aop(lambda e, t_sp1=t_sp1, t_sp2=t_sp2: e.activation(
                    out=t_sp1[:], in_=t_sp2[:], func=AF.Abs),
                    reads=(tb2,), writes=(tb1,))
                aop(lambda e, t_sp1=t_sp1: e.activation(
                    out=t_sp1[:], in_=t_sp1[:], func=AF.Exp, scale=-1.0),
                    reads=(tb1,), writes=(tb1,))
                aop(lambda e, t_sp1=t_sp1: e.activation(
                    out=t_sp1[:], in_=t_sp1[:], func=AF.Ln, bias=1.0),
                    reads=(tb1,), writes=(tb1,))
                aop(lambda e, t_sp2=t_sp2: e.activation(
                    out=t_sp2[:], in_=t_sp2[:], func=AF.Relu),
                    reads=(tb2,), writes=(tb2,))
                vop(lambda e, dc=dc, t_sp1=t_sp1, t_sp2=t_sp2: e.tensor_tensor(
                    out=d_sb[:, dc, :], in0=t_sp1[:], in1=t_sp2[:], op=ALU.add),
                    reads=(tb1, tb2), writes=(bf("d_sb"),))
            st(dv[:, :, tok0 : tok0 + c.T1], d_sb[:], "st_d", bf("d_sb"), wb=(bf("sc_d"),))

    K.barrier()
    es_p1b.close()
    es_p1.close()

    es_2a = ExitStack()
    seln = sbuf(es_2a, "seln", (c.N, c.N * P), BF16)
    d2 = sbuf(es_2a, "d2", (P, c.DINKC, c.T2), BF16)
    u2 = sbuf(es_2a, "u2", (P, c.DINKC, c.T2), BF16)
    du2 = sbuf(es_2a, "du2", (P, c.DINKC, c.T2), BF16)
    zb_sb = sbuf(es_2a, "zb_sb", (P, c.DHALF, c.T2), BF16)
    B2 = sbuf(es_2a, "B2", (c.N, c.T2), BF16)
    C2 = sbuf(es_2a, "C2", (c.N, c.T2), BF16)
    Bbs = [sbuf(es_2a, f"Bb{i}", (P, c.GN * c.T2), BF16) for i in range(2)]
    Cbs = [sbuf(es_2a, f"Cb{i}", (P, c.GN * c.T2), BF16) for i in range(2)]
    a_bigs = [sbuf(es_2a, f"a_big{i}", (P, c.GN * SEG), BF16) for i in range(2)]
    b_bigs = [sbuf(es_2a, f"b_big{i}", (P, c.GN * SEG), BF16) for i in range(2)]
    h_bigs = [sbuf(es_2a, f"h_big{i}", (P, c.GN * SEG), BF16) for i in range(2)]
    ch_bigs = [sbuf(es_2a, f"ch_big{i}", (P, c.GN * c.T2), BF16) for i in range(2)]
    y_sb = sbuf(es_2a, "y_sb", (P, c.DHALF, c.T2))
    ygts = [sbuf(es_2a, f"ygt{i}", (P, c.T2)) for i in range(2)]

    ld(seln[:], seln_in, "ld_c", bf("seln"))
    for i in range(2):
        a3i = a_bigs[i][:].rearrange("p (g s) -> p g s", g=c.GN)
        vop(lambda e, a3i=a3i: e.memset(a3i[:, :, 0:1], 0.0),
            writes=(bf(f"a_big{i}"),))

    nch2 = c.L // c.T2
    NDB = c.DINKC // c.DHALF
    for b in range(c.B):
        vop(lambda e: e.memset(h_st[:], 0.0), reads=(bf("h_st"),),
            writes=(bf("h_st"),))
        for cc2 in range(nch2):
            tok0 = b * c.L + cc2 * c.T2
            ld(d2[:], dv[:, :, tok0 : tok0 + c.T2], "ld_d", bf("d2"),
               rb=(bf("sc_d"),))
            ld(u2[:], uv[:, :, tok0 : tok0 + c.T2], "ld_u", bf("u2"),
               rb=(bf("sc_u"),))
            ld(B2[:], sc_B[:, tok0 : tok0 + c.T2], "ld_B", bf("B2"),
               rb=(bf("sc_B"),))
            ld(C2[:], sc_C[:, tok0 : tok0 + c.T2], "ld_C", bf("C2"),
               rb=(bf("sc_C"),))
            for dc in range(c.DINKC):
                vop(lambda e, dc=dc: e.tensor_tensor(
                    out=du2[:, dc, :], in0=d2[:, dc, :], in1=u2[:, dc, :],
                    op=ALU.mult),
                    reads=(bf("d2"), bf("u2")), writes=(bf("du2"),))
            pending_yadd = [None]
            iters = [(db, g, dcl) for db in range(NDB) for g in range(NG)
                     for dcl in range(c.DHALF)]

            def emit_carry(k):
                db_, g_, dcl_ = iters[k]
                dc_ = db_ * c.DHALF + dcl_
                pp_ = dcl_ % 2
                b3_ = b_bigs[pp_][:].rearrange("p (g s) -> p g s", g=c.GN)
                hsl_ = h_st[:, dc_ * c.N + g_ * c.GN : dc_ * c.N + (g_ + 1) * c.GN]
                K.op(pool, lambda e, hsl_=hsl_, b3_=b3_: e.tensor_copy(
                    b3_[:, :, 0:1], hsl_.unsqueeze(2)),
                    reads=(bf("h_st"),), writes=(bf(f"b_big{pp_}"),))

            def emit_bcast(g):
                gp = g % 2
                Bb, Cb = Bbs[gp], Cbs[gp]
                bbb, cbb = bf(f"Bb{gp}"), bf(f"Cb{gp}")
                for j in range(c.GN):
                    n = g * c.GN + j
                    p = pm[j % 2]
                    pb = bf(f"pm{j % 2}")
                    K.op(pe, lambda e, p=p, n=n: e.matmul(
                        p[:, : c.T2], seln[:, n * 128 : (n + 1) * 128], B2[:],
                        start=True, stop=True),
                        reads=(bf("seln"), bf("B2")), writes=(pb,))
                    aop(lambda e, p=p, j=j, Bb=Bb: e.activation(
                        out=Bb[:, j * c.T2 : (j + 1) * c.T2],
                        in_=p[:, : c.T2], func=AF.Copy),
                        reads=(pb,), writes=(bbb,))
                    p2_ = pm[2 + j % 2]
                    pb2 = bf(f"pm{2 + j % 2}")
                    K.op(pe, lambda e, p2_=p2_, n=n: e.matmul(
                        p2_[:, : c.T2], seln[:, n * 128 : (n + 1) * 128], C2[:],
                        start=True, stop=True),
                        reads=(bf("seln"), bf("C2")), writes=(pb2,))
                    aop(lambda e, p2_=p2_, j=j, Cb=Cb: e.activation(
                        out=Cb[:, j * c.T2 : (j + 1) * c.T2],
                        in_=p2_[:, : c.T2], func=AF.Copy),
                        reads=(pb2,), writes=(cbb,))

            def emit_epilogue(db):
                if pending_yadd[0] is not None:
                    pending_yadd[0]()
                    pending_yadd[0] = None
                ld(zb_sb[:], zv[:, db * c.DHALF : (db + 1) * c.DHALF,
                                tok0 : tok0 + c.T2], "ld_z", bf("zb"),
                   rb=(bf("sc_z"),))
                for dcl in range(c.DHALF):
                    dc = db * c.DHALF + dcl
                    ygt = ygts[dcl % 2]
                    ygb = bf(f"ygt{dcl % 2}")
                    aop(lambda e, dcl=dcl: e.activation(
                        out=zb_sb[:, dcl, :], in_=zb_sb[:, dcl, :], func=AF.Silu),
                        reads=(bf("zb"),), writes=(bf("zb"),))
                    vop(lambda e, dc=dc, dcl=dcl, ygt=ygt: e.scalar_tensor_tensor(
                        out=ygt[:], in0=u2[:, dc, :],
                        scalar=dv_c[:, dc : dc + 1], in1=y_sb[:, dcl, :],
                        op0=ALU.mult, op1=ALU.add),
                        reads=(bf("u2"), bf("dv"), bf("y_sb")), writes=(ygb,))
                    vop(lambda e, dc=dc, dcl=dcl, ygt=ygt: e.tensor_tensor(
                        out=d2[:, dc, :], in0=ygt[:], in1=zb_sb[:, dcl, :],
                        op=ALU.mult),
                        reads=(ygb, bf("zb")), writes=(bf("d2"),))

            emit_carry(0)
            NGRP = NDB * NG
            for k, (db, g, dcl) in enumerate(iters):
                gi = db * NG + g
                if dcl == 0:
                    if g == 0 and db > 0:
                        emit_epilogue(db - 1)
                    if gi == 0:
                        emit_bcast(0)
                if dcl == c.DHALF // 2 and gi + 1 < NGRP:
                    emit_bcast((gi + 1) % NG)
                gp = g % 2
                Bb, Cb = Bbs[gp], Cbs[gp]
                bbb, cbb = bf(f"Bb{gp}"), bf(f"Cb{gp}")
                dc = db * c.DHALF + dcl
                pp = dcl % 2
                a_big, b_big = a_bigs[pp], b_bigs[pp]
                hh_big, ch_big = h_bigs[pp], ch_bigs[pp]
                ab, bb_ = bf(f"a_big{pp}"), bf(f"b_big{pp}")
                hb, chb = bf(f"h_big{pp}"), bf(f"ch_big{pp}")
                b3 = b_big[:].rearrange("p (g s) -> p g s", g=c.GN)
                h3v = hh_big[:].rearrange("p (g s) -> p g s", g=c.GN)
                for j in range(c.GN):
                    n = g * c.GN + j
                    aop(lambda e, dc=dc, n=n, j=j, a_big=a_big: e.activation(
                        out=a_big[:, j * SEG + 1 : (j + 1) * SEG],
                        in_=d2[:, dc, :], func=AF.Exp,
                        scale=a_sb[:, dc * c.N + n : dc * c.N + n + 1]),
                        reads=(bf("d2"), bf("a_sb")), writes=(ab,))
                hsl = h_st[:, dc * c.N + g * c.GN : dc * c.N + (g + 1) * c.GN]
                durep = du2[:, dc, :].unsqueeze(1).broadcast_to(
                    (P, c.GN, c.T2))
                bbv = Bb[:].rearrange("p (g t) -> p g t", g=c.GN)
                vop(lambda e, durep=durep, bbv=bbv, b3=b3: e.tensor_tensor(
                    out=b3[:, :, 1 : SEG], in0=durep, in1=bbv, op=ALU.mult),
                    reads=(bf("du2"), bbb), writes=(bb_,))
                vop(lambda e, hh_big=hh_big, a_big=a_big, b_big=b_big:
                    e.tensor_tensor_scan(
                        out=hh_big[:], data0=a_big[:], data1=b_big[:],
                        initial=0.0, op0=ALU.mult, op1=ALU.add),
                    reads=(ab, bb_), writes=(hb,))
                vop(lambda e, hsl=hsl, h3v=h3v: e.tensor_copy(
                    hsl.unsqueeze(2), h3v[:, :, SEG - 1 : SEG]),
                    reads=(hb,), writes=(bf("h_st"),))
                if k + 1 < len(iters):
                    emit_carry(k + 1)
                cbv = Cb[:].rearrange("p (g t) -> p g t", g=c.GN)
                chv = ch_big[:].rearrange("p (g t) -> p g t", g=c.GN)
                vop(lambda e, cbv=cbv, chv=chv, h3v=h3v:
                    e.tensor_tensor(
                        out=chv, in0=h3v[:, :, 1 : SEG], in1=cbv,
                        op=ALU.mult),
                    reads=(hb, cbb), writes=(chb,))
                pyt = py[0]
                pyb = bf("py0")
                def ysum(e, pyt=pyt, ch_big=ch_big):
                    last = None
                    for j in range(c.GN):
                        last = e.matmul(
                            pyt[:, : c.T2], ident[:],
                            ch_big[:, j * c.T2 : (j + 1) * c.T2],
                            start=(j == 0), stop=(j == c.GN - 1))
                    return last
                if pending_yadd[0] is not None:
                    pending_yadd[0]()
                K.op(pe, ysum, reads=(chb, bf("ident")), writes=(pyb,))
                if g == 0:
                    pending_yadd[0] = (
                        lambda pyt=pyt, pyb=pyb, dcl=dcl: vop(
                            lambda e: e.tensor_copy(
                                y_sb[:, dcl, :], pyt[:, : c.T2]),
                            reads=(pyb,), writes=(bf("y_sb"),)))
                else:
                    pending_yadd[0] = (
                        lambda pyt=pyt, pyb=pyb, dcl=dcl: vop(
                            lambda e: e.tensor_tensor(
                                out=y_sb[:, dcl, :],
                                in0=y_sb[:, dcl, :],
                                in1=pyt[:, : c.T2], op=ALU.add),
                            reads=(pyb, bf("y_sb")),
                            writes=(bf("y_sb"),)))
            emit_epilogue(NDB - 1)
            st(ygv[:, :, tok0 : tok0 + c.T2], d2[:], "st_yg", bf("d2"),
               wb=(bf("sc_yg"),))

    K.barrier()
    es_2a.close()

    es_2b = ExitStack()
    w_out_sb = sbuf(es_2b, "w_out_sb", (P, c.DINKC, c.D), BF16)
    ntt2 = c.T2B // 128
    yg2 = sbuf(es_2b, "yg2", (P, c.DINKC, c.T2B), BF16)
    x2 = sbuf(es_2b, "x2", (P, ntt2, c.D))
    om_bf = sbuf(es_2b, "om_bf", (P, c.DKC, c.T2B), BF16)
    om_tm = sbuf(es_2b, "om_tm", (P, ntt2, c.D), BF16)
    x1st = sbuf(es_2b, "x1st", (P, ntt2, c.D))
    msum = sbuf(es_2b, "msum", (P, c.D))

    ld(w_out_sb[:], woutv[:], "ld_wout", bf("w_out_sb"), rb=(bf("sc_wout"),))
    nch2b = c.NT // c.T2B
    for cc_ in range(nch2b):
        tok0 = cc_ * c.T2B
        b = tok0 // c.L
        tl0 = (tok0 % c.L) // 128
        ld(yg2[:], ygv[:, :, tok0 : tok0 + c.T2B], "ld_yg", bf("yg2"),
           rb=(bf("sc_yg"),))
        ld(x2[:], xv[b, :, tl0 : tl0 + ntt2, :], "ld_x", bf("x2"))
        for oc in range(c.DKC):
            p = pm[oc % 2]
            pb = bf(f"pm{oc % 2}")
            pairs = [(w_out_sb[:, kc, oc * 128 : (oc + 1) * 128], yg2[:, kc, :])
                     for kc in range(c.DINKC)]
            mmacc(p[:, : c.T2B], pairs, (bf("w_out_sb"), bf("yg2")), pb)
            vop(lambda e, p=p, oc=oc: e.tensor_copy(om_bf[:, oc, :], p[:, : c.T2B]),
                reads=(pb,), writes=(bf("om_bf"),))
        for oc in range(c.DKC):
            for tt in range(ntt2):
                i = (oc * ntt2 + tt) % 2
                pt = ptr[i]
                pbuf = bf(f"ptr{i}")
                transp(pt[:], om_bf[:, oc, tt * 128 : (tt + 1) * 128],
                       (bf("om_bf"),), pbuf)
                vop(lambda e, pt=pt, tt=tt, oc=oc: e.tensor_copy(
                    om_tm[:, tt, oc * 128 : (oc + 1) * 128], pt[:]),
                    reads=(pbuf,), writes=(bf("om_tm"),))
        for tt in range(ntt2):
            idx = tok0 // 128 + tt
            x_t = x2[:, tt, :]
            vop(lambda e, x_t=x_t, idx=idx: e.tensor_scalar(
                out=msum[:], in0=x_t, scalar1=mu_all[:, idx : idx + 1],
                scalar2=r1_all[:, idx : idx + 1], op0=ALU.subtract, op1=ALU.mult),
                reads=(bf("x2"), bf("mu_all"), bf("r1_all")), writes=(bf("msum"),))
            vop(lambda e, tt=tt: e.tensor_tensor(
                out=msum[:], in0=msum[:], in1=om_tm[:, tt, :], op=ALU.add),
                reads=(bf("msum"), bf("om_tm")), writes=(bf("msum"),))
            vop(lambda e: e.reduce_sum(out=colA[:], in_=msum[:], axis=AX.X),
                reads=(bf("msum"),), writes=(bf("colA"),))
            aop(lambda e: e.activation(out=sqjunk[:, : c.D], in_=msum[:],
                                       func=AF.Square, accum_out=colB[:]),
                reads=(bf("msum"),), writes=(bf("colB"), bf("sqjunk")))
            vop(lambda e: e.tensor_scalar(out=colA[:], in0=colA[:],
                                          scalar1=1.0 / c.D, scalar2=None,
                                          op0=ALU.mult),
                reads=(bf("colA"),), writes=(bf("colA"),))
            vop(lambda e: e.tensor_scalar(out=colC[:], in0=colB[:],
                                          scalar1=1.0 / c.D, scalar2=None,
                                          op0=ALU.mult),
                reads=(bf("colB"),), writes=(bf("colC"),))
            vop(lambda e: e.scalar_tensor_tensor(
                out=colD[:], in0=colA[:], scalar=colA[:], in1=colC[:],
                op0=ALU.mult, op1=ALU.subtract),
                reads=(bf("colA"), bf("colC")), writes=(bf("colD"),))
            vop(lambda e: e.tensor_scalar(out=colD[:], in0=colD[:], scalar1=-1.0,
                                          scalar2=None, op0=ALU.mult),
                reads=(bf("colD"),), writes=(bf("colD"),))
            aop(lambda e: e.activation(out=colC[:], in_=colD[:], func=AF.Sqrt,
                                       bias=eps_col[:]),
                reads=(bf("colD"), bf("eps")), writes=(bf("colC"),))
            vop(lambda e: e.reciprocal(out=colC[:], in_=colC[:]),
                reads=(bf("colC"),), writes=(bf("colC"),))
            vop(lambda e: e.tensor_scalar(
                out=msum[:], in0=msum[:], scalar1=colA[:], scalar2=colC[:],
                op0=ALU.subtract, op1=ALU.mult),
                reads=(bf("msum"), bf("colA"), bf("colC")), writes=(bf("msum"),))
            vop(lambda e, x_t=x_t, tt=tt: e.tensor_tensor(
                out=x1st[:, tt, :], in0=x_t, in1=msum[:], op=ALU.add),
                reads=(bf("x2"), bf("msum")), writes=(bf("x1st"),))
            vop(lambda e, tt=tt: e.reduce_sum(out=colA[:], in_=x1st[:, tt, :],
                                              axis=AX.X),
                reads=(bf("x1st"),), writes=(bf("colA"),))
            aop(lambda e, tt=tt: e.activation(out=sqjunk[:, : c.D],
                                              in_=x1st[:, tt, :], func=AF.Square,
                                              accum_out=colB[:]),
                reads=(bf("x1st"),), writes=(bf("colB"), bf("sqjunk")))
            vop(lambda e, idx=idx: e.tensor_scalar(
                out=muc_all[:, idx : idx + 1], in0=colA[:], scalar1=1.0 / c.D,
                scalar2=None, op0=ALU.mult),
                reads=(bf("colA"),), writes=(bf("muc_all"),))
            vop(lambda e: e.tensor_scalar(out=colC[:], in0=colB[:],
                                          scalar1=1.0 / c.D, scalar2=None,
                                          op0=ALU.mult),
                reads=(bf("colB"),), writes=(bf("colC"),))
            vop(lambda e, idx=idx: e.scalar_tensor_tensor(
                out=colD[:], in0=muc_all[:, idx : idx + 1],
                scalar=muc_all[:, idx : idx + 1], in1=colC[:],
                op0=ALU.mult, op1=ALU.subtract),
                reads=(bf("muc_all"), bf("colC")), writes=(bf("colD"),))
            vop(lambda e: e.tensor_scalar(out=colD[:], in0=colD[:], scalar1=-1.0,
                                          scalar2=None, op0=ALU.mult),
                reads=(bf("colD"),), writes=(bf("colD"),))
            aop(lambda e: e.activation(out=colC[:], in_=colD[:], func=AF.Sqrt,
                                       bias=eps_col[:]),
                reads=(bf("colD"), bf("eps")), writes=(bf("colC"),))
            vop(lambda e, idx=idx: e.reciprocal(out=rc_all[:, idx : idx + 1],
                                                in_=colC[:]),
                reads=(bf("colC"),), writes=(bf("rc_all"),))
        st(x1v[:, tok0 // 128 : tok0 // 128 + ntt2, :], x1st[:], "st_x1",
           bf("x1st"), wb=(bf("sc_x1"),))

    K.barrier()
    es_2b.close()

    es_3a = ExitStack()
    sw1_sb = sbuf(es_3a, "sw1_sb", (P, c.DKC, c.HS), BF16)
    sw3_sb = sbuf(es_3a, "sw3_sb", (P, c.DKC, c.HS), BF16)
    sw2_sb = sbuf(es_3a, "sw2_sb", (P, c.HS // 128, c.D), BF16)
    ntt3 = c.T3 // 128
    x13 = sbuf(es_3a, "x13", (P, ntt3, c.D))
    xn3_tm = sbuf(es_3a, "xn3_tm", (P, ntt3, c.D), BF16)
    xn3T = sbuf(es_3a, "xn3T", (P, c.DKC, c.T3), BF16)
    h1_sb = sbuf(es_3a, "h1_sb", (P, c.HS // 128, c.T3), BF16)
    h3_sb = sbuf(es_3a, "h3_sb", (P, c.HS // 128, c.T3), BF16)
    sh_sb = sbuf(es_3a, "sh_sb", (P, c.DKC, c.T3), BF16)
    sT = sbuf(es_3a, "sT", (P, c.E))
    selT = sbuf(es_3a, "selT", (P, c.E))
    mk1 = sbuf(es_3a, "mk1", (P, c.E))
    mk2 = sbuf(es_3a, "mk2", (P, c.E))
    tmp8 = sbuf(es_3a, "tmp8", (P, c.E))
    ctb = sbuf(es_3a, "ctb", (P, c.E), BF16)
    combst = sbuf(es_3a, "combst", (c.E, c.T3), BF16)

    ld(sw1_sb[:], sw1v[:], "ld_sw1", bf("sw1_sb"), rb=(bf("sc_sw1"),))
    ld(sw3_sb[:], sw3v[:], "ld_sw3", bf("sw3_sb"), rb=(bf("sc_sw3"),))
    ld(sw2_sb[:], sw2v[:], "ld_sw2", bf("sw2_sb"), rb=(bf("sc_sw2"),))

    nch3 = c.NT // c.T3
    for cc_ in range(nch3):
        tok0 = cc_ * c.T3
        ld(x13[:], x1v[:, tok0 // 128 : tok0 // 128 + ntt3, :], "ld_x1",
           bf("x13"), rb=(bf("sc_x1"),))
        for tt in range(ntt3):
            idx = tok0 // 128 + tt
            vop(lambda e, tt=tt, idx=idx: e.tensor_scalar(
                out=xn3_tm[:, tt, :], in0=x13[:, tt, :],
                scalar1=muc_all[:, idx : idx + 1], scalar2=rc_all[:, idx : idx + 1],
                op0=ALU.subtract, op1=ALU.mult),
                reads=(bf("x13"), bf("muc_all"), bf("rc_all")),
                writes=(bf("xn3_tm"),))
        for tt in range(ntt3):
            for dkc in range(c.DKC):
                i = (tt * c.DKC + dkc) % 2
                pt = ptr[i]
                pbuf = bf(f"ptr{i}")
                transp(pt[:], xn3_tm[:, tt, dkc * 128 : (dkc + 1) * 128],
                       (bf("xn3_tm"),), pbuf)
                vop(lambda e, pt=pt, dkc=dkc, tt=tt: e.tensor_copy(
                    xn3T[:, dkc, tt * 128 : (tt + 1) * 128], pt[:]),
                    reads=(pbuf,), writes=(bf("xn3T"),))
        st(xn3v[:, :, tok0 : tok0 + c.T3], xn3T[:], "st_xn3", bf("xn3T"),
           wb=(bf("sc_xn3"),))
        for hc in range(c.HS // 128):
            p = pm[hc % 2]
            pb = bf(f"pm{hc % 2}")
            pairs = [(sw1_sb[:, kc, hc * 128 : (hc + 1) * 128], xn3T[:, kc, :])
                     for kc in range(c.DKC)]
            mmacc(p[:, : c.T3], pairs, (bf("sw1_sb"), bf("xn3T")), pb)
            aop(lambda e, p=p, hc=hc: e.activation(
                out=h1_sb[:, hc, :], in_=p[:, : c.T3], func=AF.Silu),
                reads=(pb,), writes=(bf("h1_sb"),))
            p2_ = pm[2 + hc % 2]
            pb2 = bf(f"pm{2 + hc % 2}")
            pairs = [(sw3_sb[:, kc, hc * 128 : (hc + 1) * 128], xn3T[:, kc, :])
                     for kc in range(c.DKC)]
            mmacc(p2_[:, : c.T3], pairs, (bf("sw3_sb"), bf("xn3T")), pb2)
            aop(lambda e, p2_=p2_, hc=hc: e.activation(
                out=h3_sb[:, hc, :], in_=p2_[:, : c.T3], func=AF.Copy),
                reads=(pb2,), writes=(bf("h3_sb"),))
            vop(lambda e, hc=hc: e.tensor_tensor(
                out=h3_sb[:, hc, :], in0=h1_sb[:, hc, :], in1=h3_sb[:, hc, :],
                op=ALU.mult),
                reads=(bf("h1_sb"), bf("h3_sb")), writes=(bf("h3_sb"),))
        for oc in range(c.DKC):
            p = pm[oc % 2]
            pb = bf(f"pm{oc % 2}")
            pairs = [(sw2_sb[:, hc, oc * 128 : (oc + 1) * 128], h3_sb[:, hc, :])
                     for hc in range(c.HS // 128)]
            mmacc(p[:, : c.T3], pairs, (bf("sw2_sb"), bf("h3_sb")), pb)
            vop(lambda e, p=p, oc=oc: e.tensor_copy(sh_sb[:, oc, :], p[:, : c.T3]),
                reads=(pb,), writes=(bf("sh_sb"),))
        st(shv[:, :, tok0 : tok0 + c.T3], sh_sb[:], "st_sh", bf("sh_sb"),
           wb=(bf("sc_sh"),))
        for tt in range(ntt3):
            psr = psm if tt % 2 == 0 else py[0]
            psrb = bf("psm" if tt % 2 == 0 else "py0")
            pairs = [(xn3T[:, kc, tt * 128 : (tt + 1) * 128],
                      rw_bf[:, kc * c.E : (kc + 1) * c.E]) for kc in range(c.DKC)]
            mmacc(psr[:, : c.E], pairs, (bf("xn3T"), bf("rw")), psrb)
            aop(lambda e, psr=psr: e.activation(out=sT[:], in_=psr[:, : c.E],
                                                func=AF.Sigmoid),
                reads=(psrb,), writes=(bf("sT"),))
            vop(lambda e: e.tensor_tensor(out=selT[:], in0=sT[:], in1=biasbc[:],
                                          op=ALU.add),
                reads=(bf("sT"), bf("biasbc")), writes=(bf("selT"),))
            vop(lambda e: e.reduce_max(out=colA[:], in_=selT[:], axis=AX.X),
                reads=(bf("selT"),), writes=(bf("colA"),))
            vop(lambda e: e.tensor_scalar(out=mk1[:], in0=selT[:], scalar1=colA[:],
                                          scalar2=None, op0=ALU.is_equal),
                reads=(bf("selT"), bf("colA")), writes=(bf("mk1"),))
            vop(lambda e: e.scalar_tensor_tensor(
                out=tmp8[:], in0=mk1[:], scalar=-1e30, in1=selT[:],
                op0=ALU.mult, op1=ALU.add),
                reads=(bf("mk1"), bf("selT")), writes=(bf("tmp8"),))
            vop(lambda e: e.reduce_max(out=colB[:], in_=tmp8[:], axis=AX.X),
                reads=(bf("tmp8"),), writes=(bf("colB"),))
            vop(lambda e: e.tensor_scalar(out=mk2[:], in0=tmp8[:], scalar1=colB[:],
                                          scalar2=None, op0=ALU.is_equal),
                reads=(bf("tmp8"), bf("colB")), writes=(bf("mk2"),))
            vop(lambda e: e.tensor_tensor(out=tmp8[:], in0=mk1[:], in1=sT[:],
                                          op=ALU.mult),
                reads=(bf("mk1"), bf("sT")), writes=(bf("tmp8"),))
            vop(lambda e: e.reduce_sum(out=colA[:], in_=tmp8[:], axis=AX.X),
                reads=(bf("tmp8"),), writes=(bf("colA"),))
            vop(lambda e: e.tensor_tensor(out=tmp8[:], in0=mk2[:], in1=sT[:],
                                          op=ALU.mult),
                reads=(bf("mk2"), bf("sT")), writes=(bf("tmp8"),))
            vop(lambda e: e.reduce_sum(out=colB[:], in_=tmp8[:], axis=AX.X),
                reads=(bf("tmp8"),), writes=(bf("colB"),))
            vop(lambda e: e.tensor_tensor(out=colC[:], in0=colA[:], in1=colB[:],
                                          op=ALU.add),
                reads=(bf("colA"), bf("colB")), writes=(bf("colC"),))
            vop(lambda e: e.tensor_scalar(out=colD[:], in0=colC[:], scalar1=1e-9,
                                          scalar2=None, op0=ALU.add),
                reads=(bf("colC"),), writes=(bf("colD"),))
            vop(lambda e: e.reciprocal(out=colD[:], in_=colD[:]),
                reads=(bf("colD"),), writes=(bf("colD"),))
            vop(lambda e: e.tensor_tensor(out=colA[:], in0=colA[:], in1=colD[:],
                                          op=ALU.mult),
                reads=(bf("colA"), bf("colD")), writes=(bf("colA"),))
            vop(lambda e: e.tensor_tensor(out=colB[:], in0=colB[:], in1=colD[:],
                                          op=ALU.mult),
                reads=(bf("colB"), bf("colD")), writes=(bf("colB"),))
            vop(lambda e: e.tensor_scalar(out=colE[:], in0=colC[:], scalar1=1e-9,
                                          scalar2=None, op0=ALU.is_gt),
                reads=(bf("colC"),), writes=(bf("colE"),))
            vop(lambda e: e.tensor_scalar(out=colF[:], in0=colA[:], scalar1=0.5,
                                          scalar2=None, op0=ALU.subtract),
                reads=(bf("colA"),), writes=(bf("colF"),))
            vop(lambda e: e.scalar_tensor_tensor(
                out=colF[:], in0=colF[:], scalar=0.5, in1=colE[:],
                op0=ALU.bypass, op1=ALU.mult),
                reads=(bf("colF"), bf("colE")), writes=(bf("colF"),))
            vop(lambda e: e.tensor_scalar(out=colF[:], in0=colF[:], scalar1=0.5,
                                          scalar2=None, op0=ALU.add),
                reads=(bf("colF"),), writes=(bf("colF"),))
            vop(lambda e: e.tensor_scalar(out=colG[:], in0=colB[:], scalar1=0.5,
                                          scalar2=None, op0=ALU.subtract),
                reads=(bf("colB"),), writes=(bf("colG"),))
            vop(lambda e: e.scalar_tensor_tensor(
                out=colG[:], in0=colG[:], scalar=0.5, in1=colE[:],
                op0=ALU.bypass, op1=ALU.mult),
                reads=(bf("colG"), bf("colE")), writes=(bf("colG"),))
            vop(lambda e: e.tensor_scalar(out=colG[:], in0=colG[:], scalar1=0.5,
                                          scalar2=None, op0=ALU.add),
                reads=(bf("colG"),), writes=(bf("colG"),))
            vop(lambda e: e.tensor_scalar(out=tmp8[:], in0=mk1[:], scalar1=colF[:],
                                          scalar2=None, op0=ALU.mult),
                reads=(bf("mk1"), bf("colF")), writes=(bf("tmp8"),))
            vop(lambda e: e.scalar_tensor_tensor(
                out=ctb[:], in0=mk2[:], scalar=colG[:], in1=tmp8[:],
                op0=ALU.mult, op1=ALU.add),
                reads=(bf("mk2"), bf("colG"), bf("tmp8")), writes=(bf("ctb"),))
            i = tt % 2
            pt = ptr[i]
            pbuf = bf(f"ptr{i}")
            transp(pt[: c.E, :], ctb[:], (bf("ctb"),), pbuf)
            vop(lambda e, pt=pt, tt=tt: e.tensor_copy(
                combst[:, tt * 128 : (tt + 1) * 128], pt[: c.E, :]),
                reads=(pbuf,), writes=(bf("combst"),))
        st(sc_comb[:, tok0 : tok0 + c.T3], combst[:], "st_comb", bf("combst"),
           wb=(bf("sc_comb"),))

    K.barrier()
    es_3a.close()

    es_3b = ExitStack()
    ntt3b = c.T3B // 128
    xn3b = sbuf(es_3b, "xn3b", (P, c.DKC, c.T3B), BF16)
    shb = sbuf(es_3b, "shb", (P, c.DKC, c.T3B), BF16)
    x1b = sbuf(es_3b, "x1b", (P, ntt3b, c.D))
    ew1_sb = [sbuf(es_3b, f"ew1_sb{i}", (P, c.DKC, c.HR), BF16) for i in range(2)]
    ew2_sb = [sbuf(es_3b, f"ew2_sb{i}", (P, c.HR // 128, c.D), BF16)
              for i in range(2)]
    combb = sbuf(es_3b, "combb", (P, c.T3B), BF16)
    combch = sbuf(es_3b, "combch", (c.E, c.T3B), BF16)
    he_sb = sbuf(es_3b, "he_sb", (P, c.HR // 128, c.T3B), BF16)
    racc = sbuf(es_3b, "racc", (P, c.DKC, c.T3B))
    ffn_tm = sbuf(es_3b, "ffn_tm", (P, ntt3b, c.D), BF16)
    outst = sbuf(es_3b, "outst", (P, ntt3b, c.D))

    nch3b = c.NT // c.T3B
    for cc_ in range(nch3b):
        tok0 = cc_ * c.T3B
        b = tok0 // c.L
        tl0 = (tok0 % c.L) // 128
        ld(xn3b[:], xn3v[:, :, tok0 : tok0 + c.T3B], "ld_xn3b", bf("xn3b"),
           rb=(bf("sc_xn3"),))
        ld(shb[:], shv[:, :, tok0 : tok0 + c.T3B], "ld_shb", bf("shb"),
           rb=(bf("sc_sh"),))
        ld(x1b[:], x1v[:, tok0 // 128 : tok0 // 128 + ntt3b, :], "ld_x1b",
           bf("x1b"), rb=(bf("sc_x1"),))
        ld(combch[:], sc_comb[:, tok0 : tok0 + c.T3B], "ld_comb", bf("combch"),
           rb=(bf("sc_comb"),))
        for e_ in range(c.E):
            par = e_ % 2
            ld(ew1_sb[par][:], ew1v[e_], f"ld_ew1{par}",
               bf(f"ew1_{par}"), rb=(bf("sc_ew1"),))
            K.op(pool, lambda e, par=par, e_=e_: e.dma_start(
                out=ew2_sb[par][:], in_=ew2v[e_]),
                reads=(bf("sc_ew2"),), writes=(bf(f"ew2_{par}"),),
                chan=ch(f"ld_ew2{par}"))
            p = pm[3]
            pb = bf("pm3")
            K.op(pe, lambda e, p=p, e_=e_: e.matmul(
                p[:, : c.T3B], sele[:, e_ * 128 : (e_ + 1) * 128],
                combch[:], start=True, stop=True),
                reads=(bf("sele"), bf("combch")), writes=(pb,))
            vop(lambda e, p=p: e.tensor_copy(combb[:], p[:, : c.T3B]),
                reads=(pb,), writes=(bf("combb"),))
            for hc in range(c.HR // 128):
                p = pm[hc % 2]
                pbh = bf(f"pm{hc % 2}")
                pairs = [(ew1_sb[par][:, kc, hc * 128 : (hc + 1) * 128],
                          xn3b[:, kc, :]) for kc in range(c.DKC)]
                mmacc(p[:, : c.T3B], pairs, (bf(f"ew1_{par}"), bf("xn3b")), pbh)
                aop(lambda e, p=p, hc=hc: e.activation(
                    out=he_sb[:, hc, :], in_=p[:, : c.T3B], func=AF.Gelu),
                    reads=(pbh,), writes=(bf("he_sb"),))
                vop(lambda e, hc=hc: e.tensor_tensor(
                    out=he_sb[:, hc, :], in0=he_sb[:, hc, :], in1=combb[:],
                    op=ALU.mult),
                    reads=(bf("he_sb"), bf("combb")), writes=(bf("he_sb"),))
            for oc in range(c.DKC):
                p = pm[2 + oc % 2]
                pbo = bf(f"pm{2 + oc % 2}")
                pairs = [(ew2_sb[par][:, hc, oc * 128 : (oc + 1) * 128],
                          he_sb[:, hc, :]) for hc in range(c.HR // 128)]
                mmacc(p[:, : c.T3B], pairs, (bf(f"ew2_{par}"), bf("he_sb")), pbo)
                if e_ == 0:
                    vop(lambda e, p=p, oc=oc: e.tensor_copy(racc[:, oc, :],
                                                            p[:, : c.T3B]),
                        reads=(pbo,), writes=(bf("racc"),))
                else:
                    vop(lambda e, p=p, oc=oc: e.tensor_tensor(
                        out=racc[:, oc, :], in0=racc[:, oc, :], in1=p[:, : c.T3B],
                        op=ALU.add),
                        reads=(pbo, bf("racc")), writes=(bf("racc"),))
        for oc in range(c.DKC):
            vop(lambda e, oc=oc: e.tensor_tensor(
                out=shb[:, oc, :], in0=racc[:, oc, :], in1=shb[:, oc, :],
                op=ALU.add),
                reads=(bf("racc"), bf("shb")), writes=(bf("shb"),))
        for oc in range(c.DKC):
            for tt in range(ntt3b):
                i = (oc * ntt3b + tt) % 2
                pt = ptr[i]
                pbuf = bf(f"ptr{i}")
                transp(pt[:], shb[:, oc, tt * 128 : (tt + 1) * 128],
                       (bf("shb"),), pbuf)
                vop(lambda e, pt=pt, tt=tt, oc=oc: e.tensor_copy(
                    ffn_tm[:, tt, oc * 128 : (oc + 1) * 128], pt[:]),
                    reads=(pbuf,), writes=(bf("ffn_tm"),))
        for tt in range(ntt3b):
            f_t = ffn_tm[:, tt, :]
            vop(lambda e, f_t=f_t: e.reduce_sum(out=colA[:], in_=f_t, axis=AX.X),
                reads=(bf("ffn_tm"),), writes=(bf("colA"),))
            aop(lambda e, f_t=f_t: e.activation(out=sqjunk[:, : c.D], in_=f_t,
                                                func=AF.Square, accum_out=colB[:]),
                reads=(bf("ffn_tm"),), writes=(bf("colB"), bf("sqjunk")))
            vop(lambda e: e.tensor_scalar(out=colA[:], in0=colA[:],
                                          scalar1=1.0 / c.D, scalar2=None,
                                          op0=ALU.mult),
                reads=(bf("colA"),), writes=(bf("colA"),))
            vop(lambda e: e.tensor_scalar(out=colC[:], in0=colB[:],
                                          scalar1=1.0 / c.D, scalar2=None,
                                          op0=ALU.mult),
                reads=(bf("colB"),), writes=(bf("colC"),))
            vop(lambda e: e.scalar_tensor_tensor(
                out=colD[:], in0=colA[:], scalar=colA[:], in1=colC[:],
                op0=ALU.mult, op1=ALU.subtract),
                reads=(bf("colA"), bf("colC")), writes=(bf("colD"),))
            vop(lambda e: e.tensor_scalar(out=colD[:], in0=colD[:], scalar1=-1.0,
                                          scalar2=None, op0=ALU.mult),
                reads=(bf("colD"),), writes=(bf("colD"),))
            aop(lambda e: e.activation(out=colC[:], in_=colD[:], func=AF.Sqrt,
                                       bias=eps_col[:]),
                reads=(bf("colD"), bf("eps")), writes=(bf("colC"),))
            vop(lambda e: e.reciprocal(out=colC[:], in_=colC[:]),
                reads=(bf("colC"),), writes=(bf("colC"),))
            vop(lambda e, f_t=f_t, tt=tt: e.tensor_scalar(
                out=outst[:, tt, :], in0=f_t, scalar1=colA[:], scalar2=colC[:],
                op0=ALU.subtract, op1=ALU.mult),
                reads=(bf("ffn_tm"), bf("colA"), bf("colC")), writes=(bf("outst"),))
            vop(lambda e, tt=tt: e.tensor_tensor(
                out=outst[:, tt, :], in0=outst[:, tt, :], in1=x1b[:, tt, :],
                op=ALU.add),
                reads=(bf("outst"), bf("x1b")), writes=(bf("outst"),))
        st(ov[b, :, tl0 : tl0 + ntt3b, :], outst[:], "st_out", bf("outst"))

    K.barrier()
    es_3b.close()

    with nc.Block() as block:
        K.build(nc, block, es_g)
    es_g.close()
    return nc


def _host_layout(inputs, cfg):
    c = cfg
    P = 128
    f = np.ascontiguousarray
    d = {}
    d["W_in"] = inputs["L0_attn_W_in"]
    d["conv_w_pt"] = f(inputs["L0_attn_conv_w"].reshape(c.DINKC, P, c.CONV)
                       .transpose(1, 0, 2).reshape(P, c.DINKC * c.CONV))
    d["conv_b_pt"] = f(inputs["L0_attn_conv_b"].reshape(c.DINKC, P).T)
    d["W_xproj"] = inputs["L0_attn_W_xproj"]
    d["W_dt"] = inputs["L0_attn_W_dt"]
    d["b_dt_pt"] = f(inputs["L0_attn_b_dt"].reshape(c.DINKC, P).T)
    d["A_log_pt"] = f(inputs["L0_attn_A_log"].reshape(c.DINKC, P, c.N)
                      .transpose(1, 0, 2).reshape(P, c.DINKC * c.N))
    d["D_pt"] = f(inputs["L0_attn_D"].reshape(c.DINKC, P).T)
    d["W_out"] = inputs["L0_attn_W_out"]
    d["ln_w_pt"] = f(inputs["L0_attn_ln_w"].reshape(c.DKC, P).T)
    d["ln_b_pt"] = f(inputs["L0_attn_ln_b"].reshape(c.DKC, P).T)
    d["router_w_pt"] = f(inputs["L0_ffn_router_w"].T.reshape(c.DKC, P, c.E)
                         .transpose(1, 0, 2).reshape(P, c.DKC * c.E))
    d["router_bias_bc"] = f(np.broadcast_to(inputs["L0_ffn_router_bias"],
                                            (P, c.E)))
    d["sw1"] = inputs["L0_ffn_sw1"]
    d["sw3"] = inputs["L0_ffn_sw3"]
    d["sw2"] = inputs["L0_ffn_sw2"]
    d["ew1"] = inputs["L0_ffn_ew1"]
    d["ew2"] = inputs["L0_ffn_ew2"]
    import ml_dtypes
    d["ident_bf"] = np.eye(P, dtype=ml_dtypes.bfloat16)
    seln = np.zeros((c.N, c.N * P), dtype=ml_dtypes.bfloat16)
    for n in range(c.N):
        seln[n, n * P : (n + 1) * P] = 1.0
    d["sel_n"] = seln
    sele = np.zeros((c.E, c.E * P), dtype=ml_dtypes.bfloat16)
    for e in range(c.E):
        sele[e, e * P : (e + 1) * P] = 1.0
    d["sel_e"] = sele
    return d


def kernel(**inputs):
    cfg = FULL
    x = np.ascontiguousarray(inputs["L0_attn_x"])
    n_cores = 8
    shards = np.split(x, n_cores, axis=0)
    shared = _host_layout(inputs, cfg)
    nc = bass.Bass("TRN2", target_bir_lowering=False)
    build_core(nc, cfg)
    nc.finalize()
    in_maps = []
    for i in range(n_cores):
        m = dict(shared)
        m["x"] = shards[i]
        in_maps.append(m)
    res = run_bass_kernel_spmd(nc, in_maps, core_ids=list(range(n_cores)))
    return np.concatenate([r["out"] for r in res.results], axis=0)


# version 19 (final)
# speedup vs baseline: 1.0128x; 1.0128x over previous
import numpy as np
import concourse.bass as bass
import concourse.mybir as mybir
from concourse.bass_utils import run_bass_kernel_spmd
from contextlib import ExitStack

F32 = mybir.dt.float32
BF16 = mybir.dt.bfloat16
AF = mybir.ActivationFunctionType
ALU = mybir.AluOpType
AX = mybir.AxisListType


class Buf:
    __slots__ = ("wr", "readers", "name")

    def __init__(self, name=""):
        self.wr = None
        self.readers = []
        self.name = name


class Chan:
    __slots__ = ("name", "count", "sem", "inflight_from")

    def __init__(self, name):
        self.name = name
        self.count = 0
        self.sem = None
        self.inflight_from = None


class Stream:
    def __init__(self, name, engine_attr, dma=False):
        self.name = name
        self.engine_attr = engine_attr
        self.dma = dma
        self.ops = []
        self.watermark = {}
        self.chan = Chan(name)


class Sched:
    def __init__(self):
        self.streams = {}
        self.chans = []

    def stream(self, name, engine_attr, dma=False):
        s = Stream(name, engine_attr, dma)
        self.streams[name] = s
        if s.chan is not None:
            self.chans.append(s.chan)
        return s

    def chan(self, name):
        c = Chan(name)
        self.chans.append(c)
        return c

    def op(self, stream, fn, reads=(), writes=(), chan=None, strict=True):
        dma = chan is not None
        if dma:
            step = 16
        else:
            chan = stream.chan
            step = 1
        waits = {}

        def need(mark):
            if mark is None:
                return
            c, val = mark
            if waits.get(c, 0) < val:
                waits[c] = val

        for b in reads:
            need(b.wr)
        for b in writes:
            need(b.wr)
            for r in b.readers:
                need(r)
        if dma and strict and chan.inflight_from is not None:
            need((chan, chan.inflight_from + 16))
        final = {}
        for c, val in waits.items():
            if stream.watermark.get(c, 0) < val:
                final[c] = val
                stream.watermark[c] = val
        if dma:
            chan.inflight_from = chan.count
        chan.count += step
        mark = (chan, chan.count)
        for b in reads:
            b.readers.append(mark)
        for b in writes:
            b.wr = mark
            b.readers = []
        stream.ops.append((final, fn, chan, step))
        return mark

    def wait(self, stream, marks):
        final = {}
        for c, val in marks:
            if stream.watermark.get(c, 0) < val:
                final[c] = val
                stream.watermark[c] = val
        if final:
            stream.ops.append((final, None, None, 0))

    def barrier(self):
        marks = [(c, c.count) for c in self.chans if c.count]
        for s in self.streams.values():
            self.wait(s, marks)

    def build(self, nc, block, es):
        for c in self.chans:
            c.sem = es.enter_context(nc.semaphore(f"sem_{c.name}"))

        def make_body(s):
            def body(eng):
                for waits, fn, chan, step in s.ops:
                    for c, val in waits.items():
                        eng.wait_ge(c.sem, val)
                    if fn is None:
                        continue
                    instr = fn(eng)
                    if instr is None:
                        raise RuntimeError(f"{s.name}: fn returned no instruction")
                    instr.then_inc(chan.sem, step)
            return body

        for s in self.streams.values():
            if not s.ops:
                continue
            getattr(block, s.engine_attr)(make_body(s))


class Cfg:
    def __init__(self, B=2, L=4096, D=1024, DIN=2048, N=64, R=64, E=8,
                 HS=2048, HR=1024, T1=512, T2=512, T2B=512, T3=512, T3B=512,
                 GN=8, DHALF=8):
        self.B, self.L, self.D, self.DIN = B, L, D, DIN
        self.N, self.R, self.E, self.HS, self.HR = N, R, E, HS, HR
        self.T1, self.T2, self.T2B, self.T3, self.T3B = T1, T2, T2B, T3, T3B
        self.GN = GN
        self.DHALF = DHALF
        self.CONV = 4
        self.NT = B * L
        self.DKC = D // 128
        self.DINKC = DIN // 128
        self.XCOLS = 2 * DIN
        self.PCOLS = R + 2 * N
        assert L % T1 == 0 and L % T2 == 0 and self.NT % T2B == 0
        assert self.NT % T3 == 0 and self.NT % T3B == 0
        assert N % GN == 0 and self.DINKC % DHALF == 0


FULL = Cfg()


def build_core(nc, cfg):
    c = cfg
    P = 128
    SEG = c.T2 + 1
    NG = c.N // c.GN

    def din(name, shape, dt=F32):
        return nc.dram_tensor(name, list(shape), dt, kind="ExternalInput").ap()

    x_in = din("x", (c.B, c.L, c.D))
    w_in = din("W_in", (c.D, c.XCOLS))
    conv_w = din("conv_w_pt", (P, c.DINKC * c.CONV))
    conv_b = din("conv_b_pt", (P, c.DINKC))
    w_xp = din("W_xproj", (c.DIN, c.PCOLS))
    w_dt = din("W_dt", (c.R, c.DIN))
    b_dt = din("b_dt_pt", (P, c.DINKC))
    a_log = din("A_log_pt", (P, c.DINKC * c.N))
    d_vec = din("D_pt", (P, c.DINKC))
    w_out = din("W_out", (c.DIN, c.D))
    ln_w = din("ln_w_pt", (P, c.DKC))
    ln_b = din("ln_b_pt", (P, c.DKC))
    rw = din("router_w_pt", (P, c.DKC * c.E))
    rbias = din("router_bias_bc", (P, c.E))
    sw1 = din("sw1", (c.D, c.HS))
    sw3 = din("sw3", (c.D, c.HS))
    sw2 = din("sw2", (c.HS, c.D))
    ew1 = din("ew1", (c.E, c.D, c.HR))
    ew2 = din("ew2", (c.E, c.HR, c.D))
    ident_in = din("ident_bf", (P, P), BF16)
    seln_in = din("sel_n", (c.N, c.N * P), BF16)
    sele_in = din("sel_e", (c.E, c.E * P), BF16)

    out = nc.dram_tensor("out", [c.B, c.L, c.D], F32, kind="ExternalOutput").ap()

    def scr(name, shape, dt=BF16):
        return nc.dram_tensor(name, list(shape), dt, kind="Internal").ap()

    sc_u = scr("sc_u", (c.DIN, c.NT))
    sc_z = scr("sc_z", (c.DIN, c.NT))
    sc_d = scr("sc_d", (c.DIN, c.NT))
    sc_B = scr("sc_B", (c.N, c.NT))
    sc_C = scr("sc_C", (c.N, c.NT))
    sc_yg = scr("sc_yg", (c.DIN, c.NT))
    sc_x1 = scr("sc_x1", (c.NT, c.D), F32)
    sc_xn3 = scr("sc_xn3", (c.D, c.NT))
    sc_sh = scr("sc_sh", (c.D, c.NT))
    sc_comb = scr("sc_comb", (c.E, c.NT))
    sc_wout = scr("sc_wout", (c.DIN, c.D))
    sc_sw1 = scr("sc_sw1", (c.D, c.HS))
    sc_sw3 = scr("sc_sw3", (c.D, c.HS))
    sc_sw2 = scr("sc_sw2", (c.HS, c.D))
    sc_ew1 = scr("sc_ew1", (c.E, c.D, c.HR))
    sc_ew2 = scr("sc_ew2", (c.E, c.HR, c.D))

    xv = x_in.rearrange("b (tl p) d -> b p tl d", p=P)
    ov = out.rearrange("b (tl p) d -> b p tl d", p=P)
    uv = sc_u.rearrange("(kc p) t -> p kc t", p=P)
    zv = sc_z.rearrange("(kc p) t -> p kc t", p=P)
    dv = sc_d.rearrange("(kc p) t -> p kc t", p=P)
    ygv = sc_yg.rearrange("(kc p) t -> p kc t", p=P)
    x1v = sc_x1.rearrange("(tl p) d -> p tl d", p=P)
    xn3v = sc_xn3.rearrange("(kc p) t -> p kc t", p=P)
    shv = sc_sh.rearrange("(kc p) t -> p kc t", p=P)
    woutv = sc_wout.rearrange("(kc p) d -> p kc d", p=P)
    sw1v = sc_sw1.rearrange("(kc p) h -> p kc h", p=P)
    sw3v = sc_sw3.rearrange("(kc p) h -> p kc h", p=P)
    sw2v = sc_sw2.rearrange("(kc p) d -> p kc d", p=P)
    ew1v = sc_ew1.rearrange("e (kc p) h -> e p kc h", p=P)
    ew2v = sc_ew2.rearrange("e (kc p) d -> e p kc d", p=P)

    K = Sched()
    sp = K.stream("sp", "sync", dma=True)
    pool = K.stream("pool", "gpsimd")
    dve = K.stream("dve", "vector")
    act = K.stream("act", "scalar")
    pe = K.stream("pe", "tensor")

    CH = {}
    def ch(name):
        if name not in CH:
            CH[name] = K.chan(name)
        return CH[name]

    bufs = {}
    def bf(name):
        if name not in bufs:
            bufs[name] = Buf(name)
        return bufs[name]

    es_g = ExitStack()

    def sbuf(es, name, shape, dt=F32):
        return es.enter_context(nc.sbuf_tensor(name, list(shape), dt))

    ident = sbuf(es_g, "ident", (P, P), BF16)
    sele = sbuf(es_g, "sele", (c.E, c.E * P), BF16)
    biasbc = sbuf(es_g, "biasbc", (P, c.E))
    cw_c = sbuf(es_g, "cw_c", (P, c.DINKC * c.CONV))
    cb_c = sbuf(es_g, "cb_c", (P, c.DINKC))
    bdt_c = sbuf(es_g, "bdt_c", (P, c.DINKC))
    dv_c = sbuf(es_g, "dv_c", (P, c.DINKC))
    lnw_c = sbuf(es_g, "lnw_c", (P, c.DKC))
    lnb_c = sbuf(es_g, "lnb_c", (P, c.DKC))
    rw_bf = sbuf(es_g, "rw_bf", (P, c.DKC * c.E), BF16)
    a_sb = sbuf(es_g, "a_sb", (P, c.DINKC * c.N))
    ntl = c.NT // 128
    mu_all = sbuf(es_g, "mu_all", (P, ntl))
    r1_all = sbuf(es_g, "r1_all", (P, ntl))
    muc_all = sbuf(es_g, "muc_all", (P, ntl))
    rc_all = sbuf(es_g, "rc_all", (P, ntl))
    h_st = sbuf(es_g, "h_st", (P, c.DINKC * c.N))
    eps_col = sbuf(es_g, "eps_col", (P, 1))
    half_col = sbuf(es_g, "half_col", (P, 1))
    colA = sbuf(es_g, "colA", (P, 1))
    colB = sbuf(es_g, "colB", (P, 1))
    colC = sbuf(es_g, "colC", (P, 1))
    colD = sbuf(es_g, "colD", (P, 1))
    colE = sbuf(es_g, "colE", (P, 1))
    colF = sbuf(es_g, "colF", (P, 1))
    colG = sbuf(es_g, "colG", (P, 1))
    sqjunk = sbuf(es_g, "sqjunk", (P, max(c.D, 1024)), BF16)

    pm = [es_g.enter_context(nc.psum_tensor(f"pm{i}", [P, 512], F32)) for i in range(4)]
    ptr = [es_g.enter_context(nc.psum_tensor(f"ptr{i}", [P, P], BF16)) for i in range(2)]
    psm = es_g.enter_context(nc.psum_tensor("psm", [P, 64], F32))
    py = [es_g.enter_context(nc.psum_tensor("py0", [P, 512], F32))] * 2

    def ld(dst_ap, src_ap, chan_name, wbuf, rb=()):
        K.op(sp, lambda e, d=dst_ap, s=src_ap: e.dma_start(out=d, in_=s),
             reads=rb, writes=(wbuf,), chan=ch(chan_name))

    def st(dst_ap, src_ap, chan_name, rbuf, wb=()):
        K.op(pool, lambda e, d=dst_ap, s=src_ap: e.dma_start(out=d, in_=s),
             reads=(rbuf,), writes=wb, chan=ch(chan_name))

    def vop(fn, reads=(), writes=()):
        K.op(dve, fn, reads=reads, writes=writes)

    def aop(fn, reads=(), writes=()):
        K.op(act, fn, reads=reads, writes=writes)

    def mmacc(psum_ap, pairs, reads, wbuf):
        def fn(e):
            last = None
            n = len(pairs)
            for i, (s, m) in enumerate(pairs):
                last = e.matmul(psum_ap, s, m, start=(i == 0), stop=(i == n - 1))
            return last
        K.op(pe, fn, reads=reads, writes=(wbuf,))

    def transp(out_ap, in_ap, reads, wbuf):
        K.op(pe, lambda e: e.transpose(out_ap, in_ap, ident[:]),
             reads=tuple(reads) + (bf("ident"),), writes=(wbuf,))


    vop(lambda e: e.memset(eps_col[:], 1e-5), writes=(bf("eps"),))
    vop(lambda e: e.memset(half_col[:], 0.5), writes=(bf("half"),))
    ld(ident[:], ident_in, "ld_c", bf("ident"))
    ld(sele[:], sele_in, "ld_c", bf("sele"))
    ld(biasbc[:], rbias, "ld_c", bf("biasbc"))
    ld(cw_c[:], conv_w, "ld_c", bf("cw"))
    ld(cb_c[:], conv_b, "ld_c", bf("cb"))
    ld(bdt_c[:], b_dt, "ld_c", bf("bdt"))
    ld(dv_c[:], d_vec, "ld_c", bf("dv"))
    ld(lnw_c[:], ln_w, "ld_c", bf("lnw"))

    es_p1 = ExitStack()
    w_in_sb = sbuf(es_p1, "w_in_sb", (P, c.DKC, c.XCOLS), BF16)
    w_xp_sb = sbuf(es_p1, "w_xp_sb", (P, c.DINKC, c.PCOLS), BF16)
    w_dt_sb = sbuf(es_p1, "w_dt_sb", (c.R, c.DIN), BF16)

    es_i = ExitStack()
    stage = sbuf(es_i, "stage", (P, 4096))

    ld(lnb_c[:], ln_b, "ld_c", bf("lnb"))
    ld(stage[:, : c.DKC * c.E], rw, "ld_c", bf("stage"))
    vop(lambda e: e.tensor_copy(rw_bf[:], stage[:, : c.DKC * c.E]),
        reads=(bf("stage"),), writes=(bf("rw"),))
    ld(stage[:, : c.DINKC * c.N], a_log, "ld_c", bf("stage"), rb=(bf("rw"),))
    aop(lambda e: e.activation(out=a_sb[:], in_=stage[:, : c.DINKC * c.N],
                               func=AF.Exp),
        reads=(bf("stage"),), writes=(bf("a_sb"),))
    vop(lambda e: e.tensor_scalar(out=a_sb[:], in0=a_sb[:], scalar1=-1.0,
                                  scalar2=None, op0=ALU.mult),
        reads=(bf("a_sb"),), writes=(bf("a_sb"),))

    w_in_v = w_in.rearrange("(kc p) x -> p kc x", p=P)
    for kc in range(c.DKC):
        ld(stage[:, : c.XCOLS], w_in_v[:, kc, :], "ld_w", bf("stage"),
           rb=(bf("w_in_sb"),) if kc else (bf("a_sb"),))
        vop(lambda e, kc=kc: e.tensor_copy(w_in_sb[:, kc, :],
                                           stage[:, : c.XCOLS]),
            reads=(bf("stage"),), writes=(bf("w_in_sb"),))
    w_xp_v = w_xp.rearrange("(kc p) x -> p kc x", p=P)
    for kc in range(c.DINKC):
        ld(stage[:, : c.PCOLS], w_xp_v[:, kc, :], "ld_w", bf("stage"),
           rb=(bf("w_xp_sb"),) if kc else (bf("w_in_sb"),))
        vop(lambda e, kc=kc: e.tensor_copy(w_xp_sb[:, kc, :], stage[:, : c.PCOLS]),
            reads=(bf("stage"),), writes=(bf("w_xp_sb"),))
    ld(stage[: c.R, : c.DIN], w_dt, "ld_w", bf("stage"), rb=(bf("w_xp_sb"),))
    vop(lambda e: e.tensor_copy(w_dt_sb[:], stage[: c.R, : c.DIN]),
        reads=(bf("stage"),), writes=(bf("w_dt_sb"),))


    NSTG = 4
    stages = [sbuf(es_i, f"stage{i}", (P, 4096)) for i in range(NSTG)]
    stbfs = [sbuf(es_i, f"stbf{i}", (P, 4096), BF16) for i in range(NSTG)]
    _wc = [0]

    def wcast(src, dst, rows, cols, name):
        nch = rows // P
        for i in range(nch):
            par = _wc[0] % NSTG
            _wc[0] += 1
            stg, stb = stages[par], stbfs[par]
            sgb = bf(f"stageb{par}")
            sbb = bf(f"stbfb{par}")
            K.op(sp, lambda e, s=src, i=i, stg=stg, cols=cols: e.dma_start(
                out=stg[:, :cols], in_=s[i * P : (i + 1) * P, :]),
                reads=(sbb,), writes=(sgb,), chan=ch(f"ld_w{par}"))
            aop(lambda e, stg=stg, stb=stb, cols=cols: e.activation(
                out=stb[:, :cols], in_=stg[:, :cols], func=AF.Copy),
                reads=(sgb,), writes=(sbb,))
            K.op(pool, lambda e, d=dst, i=i, stb=stb, cols=cols: e.dma_start(
                out=d[i * P : (i + 1) * P, :], in_=stb[:, :cols]),
                reads=(sbb,), writes=(), chan=ch(f"st_w{par}"))

    wcast(w_out, sc_wout, c.DIN, c.D, "wout")
    wcast(sw1, sc_sw1, c.D, c.HS, "sw1")
    wcast(sw3, sc_sw3, c.D, c.HS, "sw3")
    wcast(sw2, sc_sw2, c.HS, c.D, "sw2")
    for e_ in range(c.E):
        wcast(ew1[e_], sc_ew1[e_], c.D, c.HR, f"ew1_{e_}")
        wcast(ew2[e_], sc_ew2[e_], c.HR, c.D, f"ew2_{e_}")

    K.barrier()
    es_i.close()

    es_p1b = ExitStack()
    ntt1 = c.T1 // 128
    x_sb = sbuf(es_p1b, "x_sb", (P, ntt1, c.D))
    xn2_tm = sbuf(es_p1b, "xn2_tm", (P, ntt1, c.D), BF16)
    xn2T = sbuf(es_p1b, "xn2T", (P, c.DKC, c.T1), BF16)
    xi_sb = sbuf(es_p1b, "xi_sb", (P, c.DINKC, c.CONV - 1 + c.T1), BF16)
    z_sb = sbuf(es_p1b, "z_sb", (P, c.DINKC, c.T1), BF16)
    u_sb = sbuf(es_p1b, "u_sb", (P, c.DINKC, c.T1), BF16)
    caccs = [sbuf(es_p1b, f"cacc{i}", (P, c.T1)) for i in range(2)]
    t_sp1s = [sbuf(es_p1b, f"t_sp1_{i}", (P, c.T1)) for i in range(2)]
    t_sp2s = [sbuf(es_p1b, f"t_sp2_{i}", (P, c.T1)) for i in range(2)]
    d_sb = sbuf(es_p1b, "d_sb", (P, c.DINKC, c.T1), BF16)
    dtr_bf = sbuf(es_p1b, "dtr_bf", (c.R, c.T1), BF16)
    B_bf = sbuf(es_p1b, "B_bf", (c.N, c.T1), BF16)
    C_bf = sbuf(es_p1b, "C_bf", (c.N, c.T1), BF16)

    nch1 = c.L // c.T1
    for b in range(c.B):
        for cc1 in range(nch1):
            tok0 = b * c.L + cc1 * c.T1
            tl0 = cc1 * ntt1
            ld(x_sb[:], xv[b, :, tl0 : tl0 + ntt1, :], "ld_x", bf("x_sb"))
            for tt in range(ntt1):
                idx = tok0 // 128 + tt
                x_t = x_sb[:, tt, :]
                vop(lambda e, x_t=x_t: e.reduce_sum(out=colA[:], in_=x_t, axis=AX.X),
                    reads=(bf("x_sb"),), writes=(bf("colA"),))
                aop(lambda e, x_t=x_t: e.activation(out=sqjunk[:, : c.D], in_=x_t,
                                                    func=AF.Square, accum_out=colB[:]),
                    reads=(bf("x_sb"),), writes=(bf("colB"), bf("sqjunk")))
                vop(lambda e, idx=idx: e.tensor_scalar(
                    out=mu_all[:, idx : idx + 1], in0=colA[:], scalar1=1.0 / c.D,
                    scalar2=None, op0=ALU.mult),
                    reads=(bf("colA"),), writes=(bf("mu_all"),))
                vop(lambda e: e.tensor_scalar(out=colC[:], in0=colB[:],
                                              scalar1=1.0 / c.D, scalar2=None,
                                              op0=ALU.mult),
                    reads=(bf("colB"),), writes=(bf("colC"),))
                vop(lambda e, idx=idx: e.scalar_tensor_tensor(
                    out=colD[:], in0=mu_all[:, idx : idx + 1],
                    scalar=mu_all[:, idx : idx + 1], in1=colC[:],
                    op0=ALU.mult, op1=ALU.subtract),
                    reads=(bf("mu_all"), bf("colC")), writes=(bf("colD"),))
                vop(lambda e: e.tensor_scalar(out=colD[:], in0=colD[:], scalar1=-1.0,
                                              scalar2=None, op0=ALU.mult),
                    reads=(bf("colD"),), writes=(bf("colD"),))
                aop(lambda e: e.activation(out=colC[:], in_=colD[:], func=AF.Sqrt,
                                           bias=eps_col[:]),
                    reads=(bf("colD"), bf("eps")), writes=(bf("colC"),))
                vop(lambda e, idx=idx: e.reciprocal(out=r1_all[:, idx : idx + 1],
                                                    in_=colC[:]),
                    reads=(bf("colC"),), writes=(bf("r1_all"),))
                vop(lambda e, idx=idx: e.tensor_tensor(
                    out=colE[:], in0=r1_all[:, idx : idx + 1],
                    in1=r1_all[:, idx : idx + 1], op=ALU.mult),
                    reads=(bf("r1_all"),), writes=(bf("colE"),))
                vop(lambda e: e.tensor_tensor(out=colE[:], in0=colD[:], in1=colE[:],
                                              op=ALU.mult),
                    reads=(bf("colD"), bf("colE")), writes=(bf("colE"),))
                aop(lambda e: e.activation(out=colE[:], in_=colE[:], func=AF.Sqrt,
                                           bias=eps_col[:]),
                    reads=(bf("colE"), bf("eps")), writes=(bf("colE"),))
                vop(lambda e: e.reciprocal(out=colE[:], in_=colE[:]),
                    reads=(bf("colE"),), writes=(bf("colE"),))
                vop(lambda e, idx=idx: e.tensor_tensor(
                    out=colE[:], in0=colE[:], in1=r1_all[:, idx : idx + 1],
                    op=ALU.mult),
                    reads=(bf("colE"), bf("r1_all")), writes=(bf("colE"),))
                vop(lambda e, x_t=x_t, tt=tt, idx=idx: e.tensor_scalar(
                    out=xn2_tm[:, tt, :], in0=x_t,
                    scalar1=mu_all[:, idx : idx + 1], scalar2=colE[:],
                    op0=ALU.subtract, op1=ALU.mult),
                    reads=(bf("x_sb"), bf("mu_all"), bf("colE")),
                    writes=(bf("xn2_tm"),))
            for tt in range(ntt1):
                for dkc in range(c.DKC):
                    pt = ptr[(tt * c.DKC + dkc) % 2]
                    pbuf = bf(f"ptr{(tt * c.DKC + dkc) % 2}")
                    transp(pt[:], xn2_tm[:, tt, dkc * 128 : (dkc + 1) * 128],
                           (bf("xn2_tm"),), pbuf)
                    vop(lambda e, pt=pt, dkc=dkc, tt=tt: e.tensor_scalar(
                        out=xn2T[:, dkc, tt * 128 : (tt + 1) * 128], in0=pt[:],
                        scalar1=lnw_c[:, dkc : dkc + 1],
                        scalar2=lnb_c[:, dkc : dkc + 1],
                        op0=ALU.mult, op1=ALU.add),
                        reads=(pbuf, bf("lnw"), bf("lnb")), writes=(bf("xn2T"),))
            for dc in range(c.DINKC):
                if cc1 == 0:
                    vop(lambda e, dc=dc: e.memset(xi_sb[:, dc, 0 : c.CONV - 1], 0.0),
                        reads=(bf("xi"),), writes=(bf("xi"),))
                else:
                    vop(lambda e, dc=dc: e.tensor_copy(
                        xi_sb[:, dc, 0 : c.CONV - 1],
                        xi_sb[:, dc, c.T1 : c.T1 + c.CONV - 1]),
                        reads=(bf("xi"),), writes=(bf("xi"),))
            for cc in range(c.XCOLS // 128):
                p = pm[cc % 2]
                pb = bf(f"pm{cc % 2}")
                pairs = [(w_in_sb[:, kc, cc * 128 : (cc + 1) * 128], xn2T[:, kc, :])
                         for kc in range(c.DKC)]
                mmacc(p[:, : c.T1], pairs, (bf("w_in_sb"), bf("xn2T")), pb)
                if cc < c.DINKC:
                    dst = xi_sb[:, cc, c.CONV - 1 : c.CONV - 1 + c.T1]
                    wb = bf("xi")
                else:
                    dst = z_sb[:, cc - c.DINKC, :]
                    wb = bf("z_sb")
                vop(lambda e, p=p, dst=dst: e.tensor_copy(dst, p[:, : c.T1]),
                    reads=(pb,), writes=(wb,))
            for dc in range(c.DINKC):
                cacc = caccs[dc % 2]
                cb_ = bf(f"cacc{dc % 2}")
                vop(lambda e, dc=dc, cacc=cacc: e.tensor_scalar(
                    out=cacc[:], in0=xi_sb[:, dc, 0 : c.T1],
                    scalar1=cw_c[:, dc * c.CONV : dc * c.CONV + 1], scalar2=None,
                    op0=ALU.mult),
                    reads=(bf("xi"), bf("cw")), writes=(cb_,))
                for k in range(1, c.CONV):
                    vop(lambda e, dc=dc, k=k, cacc=cacc: e.scalar_tensor_tensor(
                        out=cacc[:], in0=xi_sb[:, dc, k : k + c.T1],
                        scalar=cw_c[:, dc * c.CONV + k : dc * c.CONV + k + 1],
                        in1=cacc[:], op0=ALU.mult, op1=ALU.add),
                        reads=(bf("xi"), bf("cw"), cb_), writes=(cb_,))
                aop(lambda e, dc=dc, cacc=cacc: e.activation(
                    out=u_sb[:, dc, :], in_=cacc[:], func=AF.Silu,
                    bias=cb_c[:, dc : dc + 1]),
                    reads=(cb_, bf("cb")), writes=(bf("u_sb"),))
            st(uv[:, :, tok0 : tok0 + c.T1], u_sb[:], "st_u", bf("u_sb"), wb=(bf("sc_u"),))
            st(zv[:, :, tok0 : tok0 + c.T1], z_sb[:], "st_z", bf("z_sb"), wb=(bf("sc_z"),))
            nparts = (c.PCOLS + 127) // 128
            for part in range(nparts):
                cw_ = min(128, c.PCOLS - part * 128)
                p = pm[2 + part % 2]
                pb = bf(f"pm{2 + part % 2}")
                pairs = [(w_xp_sb[:, kc, part * 128 : part * 128 + cw_],
                          u_sb[:, kc, :]) for kc in range(c.DINKC)]
                mmacc(p[:cw_, : c.T1], pairs, (bf("w_xp_sb"), bf("u_sb")), pb)
                for (name, tile, wb, lo, hi) in (
                    ("dtr", dtr_bf, bf("dtr"), 0, c.R),
                    ("B", B_bf, bf("B_bf"), c.R, c.R + c.N),
                    ("C", C_bf, bf("C_bf"), c.R + c.N, c.R + 2 * c.N),
                ):
                    s0 = max(lo, part * 128)
                    s1 = min(hi, part * 128 + cw_)
                    if s0 >= s1:
                        continue
                    vop(lambda e, p=p, s0=s0, s1=s1, part=part, tile=tile, lo=lo:
                        e.tensor_copy(tile[s0 - lo : s1 - lo, :],
                                      p[s0 - part * 128 : s1 - part * 128, : c.T1]),
                        reads=(pb,), writes=(wb,))
            st(sc_B[:, tok0 : tok0 + c.T1], B_bf[:], "st_B", bf("B_bf"), wb=(bf("sc_B"),))
            st(sc_C[:, tok0 : tok0 + c.T1], C_bf[:], "st_C", bf("C_bf"), wb=(bf("sc_C"),))
            for dc in range(c.DINKC):
                p = pm[dc % 2]
                pb = bf(f"pm{dc % 2}")
                t_sp1, t_sp2 = t_sp1s[dc % 2], t_sp2s[dc % 2]
                tb1, tb2 = bf(f"t_sp1_{dc % 2}"), bf(f"t_sp2_{dc % 2}")
                pairs = [(w_dt_sb[:, dc * 128 : (dc + 1) * 128], dtr_bf[:])]
                mmacc(p[:, : c.T1], pairs, (bf("w_dt_sb"), bf("dtr")), pb)
                vop(lambda e, p=p, dc=dc, t_sp2=t_sp2: e.tensor_scalar(
                    out=t_sp2[:], in0=p[:, : c.T1],
                    scalar1=bdt_c[:, dc : dc + 1], scalar2=None, op0=ALU.add),
                    reads=(pb, bf("bdt")), writes=(tb2,))
                aop(lambda e, t_sp1=t_sp1, t_sp2=t_sp2: e.activation(
                    out=t_sp1[:], in_=t_sp2[:], func=AF.Abs),
                    reads=(tb2,), writes=(tb1,))
                aop(lambda e, t_sp1=t_sp1: e.activation(
                    out=t_sp1[:], in_=t_sp1[:], func=AF.Exp, scale=-1.0),
                    reads=(tb1,), writes=(tb1,))
                aop(lambda e, t_sp1=t_sp1: e.activation(
                    out=t_sp1[:], in_=t_sp1[:], func=AF.Ln, bias=1.0),
                    reads=(tb1,), writes=(tb1,))
                aop(lambda e, t_sp2=t_sp2: e.activation(
                    out=t_sp2[:], in_=t_sp2[:], func=AF.Relu),
                    reads=(tb2,), writes=(tb2,))
                vop(lambda e, dc=dc, t_sp1=t_sp1, t_sp2=t_sp2: e.tensor_tensor(
                    out=d_sb[:, dc, :], in0=t_sp1[:], in1=t_sp2[:], op=ALU.add),
                    reads=(tb1, tb2), writes=(bf("d_sb"),))
            st(dv[:, :, tok0 : tok0 + c.T1], d_sb[:], "st_d", bf("d_sb"), wb=(bf("sc_d"),))

    K.barrier()
    es_p1b.close()
    es_p1.close()

    es_2a = ExitStack()
    seln = sbuf(es_2a, "seln", (c.N, c.N * P), BF16)
    d2 = sbuf(es_2a, "d2", (P, c.DINKC, c.T2), BF16)
    u2 = sbuf(es_2a, "u2", (P, c.DINKC, c.T2), BF16)
    du2 = sbuf(es_2a, "du2", (P, c.DINKC, c.T2), BF16)
    zb_sb = sbuf(es_2a, "zb_sb", (P, c.DHALF, c.T2), BF16)
    B2 = sbuf(es_2a, "B2", (c.N, c.T2), BF16)
    C2 = sbuf(es_2a, "C2", (c.N, c.T2), BF16)
    Bbs = [sbuf(es_2a, f"Bb{i}", (P, c.GN * c.T2), BF16) for i in range(2)]
    Cbs = [sbuf(es_2a, f"Cb{i}", (P, c.GN * c.T2), BF16) for i in range(2)]
    a_bigs = [sbuf(es_2a, f"a_big{i}", (P, c.GN * SEG), BF16) for i in range(2)]
    b_bigs = [sbuf(es_2a, f"b_big{i}", (P, c.GN * SEG), BF16) for i in range(2)]
    h_bigs = [sbuf(es_2a, f"h_big{i}", (P, c.GN * SEG), BF16) for i in range(2)]
    ch_bigs = [sbuf(es_2a, f"ch_big{i}", (P, c.GN * c.T2), BF16) for i in range(2)]
    y_sb = sbuf(es_2a, "y_sb", (P, c.DHALF, c.T2))
    ygts = [sbuf(es_2a, f"ygt{i}", (P, c.T2)) for i in range(2)]

    ld(seln[:], seln_in, "ld_c", bf("seln"))
    for i in range(2):
        a3i = a_bigs[i][:].rearrange("p (g s) -> p g s", g=c.GN)
        vop(lambda e, a3i=a3i: e.memset(a3i[:, :, 0:1], 0.0),
            writes=(bf(f"a_big{i}"),))

    nch2 = c.L // c.T2
    NDB = c.DINKC // c.DHALF
    for b in range(c.B):
        vop(lambda e: e.memset(h_st[:], 0.0), reads=(bf("h_st"),),
            writes=(bf("h_st"),))
        for cc2 in range(nch2):
            tok0 = b * c.L + cc2 * c.T2
            H = c.DHALF
            ld(d2[:, :H, :], dv[:, :H, tok0 : tok0 + c.T2], "ld_d0", bf("d2a"),
               rb=(bf("sc_d"),))
            ld(u2[:, :H, :], uv[:, :H, tok0 : tok0 + c.T2], "ld_u0", bf("u2a"),
               rb=(bf("sc_u"),))
            ld(d2[:, H:, :], dv[:, H:, tok0 : tok0 + c.T2], "ld_d1", bf("d2b"),
               rb=(bf("sc_d"),))
            ld(u2[:, H:, :], uv[:, H:, tok0 : tok0 + c.T2], "ld_u1", bf("u2b"),
               rb=(bf("sc_u"),))
            ld(B2[:], sc_B[:, tok0 : tok0 + c.T2], "ld_B", bf("B2"),
               rb=(bf("sc_B"),))
            ld(C2[:], sc_C[:, tok0 : tok0 + c.T2], "ld_C", bf("C2"),
               rb=(bf("sc_C"),))
            for dc in range(c.DINKC):
                hb_d = bf("d2a") if dc < c.DHALF else bf("d2b")
                hb_u = bf("u2a") if dc < c.DHALF else bf("u2b")
                vop(lambda e, dc=dc: e.tensor_tensor(
                    out=du2[:, dc, :], in0=d2[:, dc, :], in1=u2[:, dc, :],
                    op=ALU.mult),
                    reads=(hb_d, hb_u), writes=(bf("du2"),))
            pending_yadd = [None]
            iters = [(db, g, dcl) for db in range(NDB) for g in range(NG)
                     for dcl in range(c.DHALF)]

            def emit_carry(k):
                db_, g_, dcl_ = iters[k]
                dc_ = db_ * c.DHALF + dcl_
                pp_ = dcl_ % 2
                b3_ = b_bigs[pp_][:].rearrange("p (g s) -> p g s", g=c.GN)
                hsl_ = h_st[:, dc_ * c.N + g_ * c.GN : dc_ * c.N + (g_ + 1) * c.GN]
                K.op(pool, lambda e, hsl_=hsl_, b3_=b3_: e.tensor_copy(
                    b3_[:, :, 0:1], hsl_.unsqueeze(2)),
                    reads=(bf("h_st"),), writes=(bf(f"b_big{pp_}"),))

            def emit_bcast(g):
                gp = g % 2
                Bb, Cb = Bbs[gp], Cbs[gp]
                bbb, cbb = bf(f"Bb{gp}"), bf(f"Cb{gp}")
                for j in range(c.GN):
                    n = g * c.GN + j
                    p = pm[j % 2]
                    pb = bf(f"pm{j % 2}")
                    K.op(pe, lambda e, p=p, n=n: e.matmul(
                        p[:, : c.T2], seln[:, n * 128 : (n + 1) * 128], B2[:],
                        start=True, stop=True),
                        reads=(bf("seln"), bf("B2")), writes=(pb,))
                    aop(lambda e, p=p, j=j, Bb=Bb: e.activation(
                        out=Bb[:, j * c.T2 : (j + 1) * c.T2],
                        in_=p[:, : c.T2], func=AF.Copy),
                        reads=(pb,), writes=(bbb,))
                    p2_ = pm[2 + j % 2]
                    pb2 = bf(f"pm{2 + j % 2}")
                    K.op(pe, lambda e, p2_=p2_, n=n: e.matmul(
                        p2_[:, : c.T2], seln[:, n * 128 : (n + 1) * 128], C2[:],
                        start=True, stop=True),
                        reads=(bf("seln"), bf("C2")), writes=(pb2,))
                    aop(lambda e, p2_=p2_, j=j, Cb=Cb: e.activation(
                        out=Cb[:, j * c.T2 : (j + 1) * c.T2],
                        in_=p2_[:, : c.T2], func=AF.Copy),
                        reads=(pb2,), writes=(cbb,))

            def emit_epilogue(db):
                if pending_yadd[0] is not None:
                    pending_yadd[0]()
                    pending_yadd[0] = None
                ld(zb_sb[:], zv[:, db * c.DHALF : (db + 1) * c.DHALF,
                                tok0 : tok0 + c.T2], "ld_z", bf("zb"),
                   rb=(bf("sc_z"),))
                for dcl in range(c.DHALF):
                    dc = db * c.DHALF + dcl
                    ygt = ygts[dcl % 2]
                    ygb = bf(f"ygt{dcl % 2}")
                    aop(lambda e, dcl=dcl: e.activation(
                        out=zb_sb[:, dcl, :], in_=zb_sb[:, dcl, :], func=AF.Silu),
                        reads=(bf("zb"),), writes=(bf("zb"),))
                    vop(lambda e, dc=dc, dcl=dcl, ygt=ygt: e.scalar_tensor_tensor(
                        out=ygt[:], in0=u2[:, dc, :],
                        scalar=dv_c[:, dc : dc + 1], in1=y_sb[:, dcl, :],
                        op0=ALU.mult, op1=ALU.add),
                        reads=(bf("u2a") if dc < c.DHALF else bf("u2b"),
                               bf("dv"), bf("y_sb")), writes=(ygb,))
                    vop(lambda e, dc=dc, dcl=dcl, ygt=ygt: e.tensor_tensor(
                        out=d2[:, dc, :], in0=ygt[:], in1=zb_sb[:, dcl, :],
                        op=ALU.mult),
                        reads=(ygb, bf("zb")),
                        writes=(bf("d2a") if dc < c.DHALF else bf("d2b"),))

            emit_carry(0)
            NGRP = NDB * NG
            for k, (db, g, dcl) in enumerate(iters):
                gi = db * NG + g
                if dcl == 0:
                    if g == 0 and db > 0:
                        emit_epilogue(db - 1)
                    if gi == 0:
                        emit_bcast(0)
                if dcl == c.DHALF // 2 and gi + 1 < NGRP:
                    emit_bcast((gi + 1) % NG)
                gp = g % 2
                Bb, Cb = Bbs[gp], Cbs[gp]
                bbb, cbb = bf(f"Bb{gp}"), bf(f"Cb{gp}")
                dc = db * c.DHALF + dcl
                pp = dcl % 2
                a_big, b_big = a_bigs[pp], b_bigs[pp]
                hh_big, ch_big = h_bigs[pp], ch_bigs[pp]
                ab, bb_ = bf(f"a_big{pp}"), bf(f"b_big{pp}")
                hb, chb = bf(f"h_big{pp}"), bf(f"ch_big{pp}")
                b3 = b_big[:].rearrange("p (g s) -> p g s", g=c.GN)
                h3v = hh_big[:].rearrange("p (g s) -> p g s", g=c.GN)
                for j in range(c.GN):
                    n = g * c.GN + j
                    aop(lambda e, dc=dc, n=n, j=j, a_big=a_big: e.activation(
                        out=a_big[:, j * SEG + 1 : (j + 1) * SEG],
                        in_=d2[:, dc, :], func=AF.Exp,
                        scale=a_sb[:, dc * c.N + n : dc * c.N + n + 1]),
                        reads=(bf("d2a") if dc < c.DHALF else bf("d2b"),
                               bf("a_sb")), writes=(ab,))
                hsl = h_st[:, dc * c.N + g * c.GN : dc * c.N + (g + 1) * c.GN]
                durep = du2[:, dc, :].unsqueeze(1).broadcast_to(
                    (P, c.GN, c.T2))
                bbv = Bb[:].rearrange("p (g t) -> p g t", g=c.GN)
                vop(lambda e, durep=durep, bbv=bbv, b3=b3: e.tensor_tensor(
                    out=b3[:, :, 1 : SEG], in0=durep, in1=bbv, op=ALU.mult),
                    reads=(bf("du2"), bbb), writes=(bb_,))
                vop(lambda e, hh_big=hh_big, a_big=a_big, b_big=b_big:
                    e.tensor_tensor_scan(
                        out=hh_big[:], data0=a_big[:], data1=b_big[:],
                        initial=0.0, op0=ALU.mult, op1=ALU.add),
                    reads=(ab, bb_), writes=(hb,))
                vop(lambda e, hsl=hsl, h3v=h3v: e.tensor_copy(
                    hsl.unsqueeze(2), h3v[:, :, SEG - 1 : SEG]),
                    reads=(hb,), writes=(bf("h_st"),))
                if k + 1 < len(iters):
                    emit_carry(k + 1)
                cbv = Cb[:].rearrange("p (g t) -> p g t", g=c.GN)
                chv = ch_big[:].rearrange("p (g t) -> p g t", g=c.GN)
                vop(lambda e, cbv=cbv, chv=chv, h3v=h3v:
                    e.tensor_tensor(
                        out=chv, in0=h3v[:, :, 1 : SEG], in1=cbv,
                        op=ALU.mult),
                    reads=(hb, cbb), writes=(chb,))
                pyt = py[0]
                pyb = bf("py0")
                def ysum(e, pyt=pyt, ch_big=ch_big):
                    last = None
                    for j in range(c.GN):
                        last = e.matmul(
                            pyt[:, : c.T2], ident[:],
                            ch_big[:, j * c.T2 : (j + 1) * c.T2],
                            start=(j == 0), stop=(j == c.GN - 1))
                    return last
                if pending_yadd[0] is not None:
                    pending_yadd[0]()
                K.op(pe, ysum, reads=(chb, bf("ident")), writes=(pyb,))
                if g == 0:
                    pending_yadd[0] = (
                        lambda pyt=pyt, pyb=pyb, dcl=dcl: vop(
                            lambda e: e.tensor_copy(
                                y_sb[:, dcl, :], pyt[:, : c.T2]),
                            reads=(pyb,), writes=(bf("y_sb"),)))
                else:
                    pending_yadd[0] = (
                        lambda pyt=pyt, pyb=pyb, dcl=dcl: vop(
                            lambda e: e.tensor_tensor(
                                out=y_sb[:, dcl, :],
                                in0=y_sb[:, dcl, :],
                                in1=pyt[:, : c.T2], op=ALU.add),
                            reads=(pyb, bf("y_sb")),
                            writes=(bf("y_sb"),)))
            emit_epilogue(NDB - 1)
            K.op(pool, lambda e, tok0=tok0: e.dma_start(
                out=ygv[:, :, tok0 : tok0 + c.T2], in_=d2[:]),
                reads=(bf("d2a"), bf("d2b")), writes=(bf("sc_yg"),),
                chan=ch("st_yg"))

    K.barrier()
    es_2a.close()

    es_2b = ExitStack()
    w_out_sb = sbuf(es_2b, "w_out_sb", (P, c.DINKC, c.D), BF16)
    ntt2 = c.T2B // 128
    yg2 = sbuf(es_2b, "yg2", (P, c.DINKC, c.T2B), BF16)
    x2 = sbuf(es_2b, "x2", (P, ntt2, c.D))
    om_bf = sbuf(es_2b, "om_bf", (P, c.DKC, c.T2B), BF16)
    om_tm = sbuf(es_2b, "om_tm", (P, ntt2, c.D), BF16)
    x1st = sbuf(es_2b, "x1st", (P, ntt2, c.D))
    msum = sbuf(es_2b, "msum", (P, c.D))

    ld(w_out_sb[:], woutv[:], "ld_wout", bf("w_out_sb"), rb=(bf("sc_wout"),))
    nch2b = c.NT // c.T2B
    for cc_ in range(nch2b):
        tok0 = cc_ * c.T2B
        b = tok0 // c.L
        tl0 = (tok0 % c.L) // 128
        ld(yg2[:], ygv[:, :, tok0 : tok0 + c.T2B], "ld_yg", bf("yg2"),
           rb=(bf("sc_yg"),))
        ld(x2[:], xv[b, :, tl0 : tl0 + ntt2, :], "ld_x", bf("x2"))
        for oc in range(c.DKC):
            p = pm[oc % 2]
            pb = bf(f"pm{oc % 2}")
            pairs = [(w_out_sb[:, kc, oc * 128 : (oc + 1) * 128], yg2[:, kc, :])
                     for kc in range(c.DINKC)]
            mmacc(p[:, : c.T2B], pairs, (bf("w_out_sb"), bf("yg2")), pb)
            vop(lambda e, p=p, oc=oc: e.tensor_copy(om_bf[:, oc, :], p[:, : c.T2B]),
                reads=(pb,), writes=(bf("om_bf"),))
        for oc in range(c.DKC):
            for tt in range(ntt2):
                i = (oc * ntt2 + tt) % 2
                pt = ptr[i]
                pbuf = bf(f"ptr{i}")
                transp(pt[:], om_bf[:, oc, tt * 128 : (tt + 1) * 128],
                       (bf("om_bf"),), pbuf)
                vop(lambda e, pt=pt, tt=tt, oc=oc: e.tensor_copy(
                    om_tm[:, tt, oc * 128 : (oc + 1) * 128], pt[:]),
                    reads=(pbuf,), writes=(bf("om_tm"),))
        for tt in range(ntt2):
            idx = tok0 // 128 + tt
            x_t = x2[:, tt, :]
            vop(lambda e, x_t=x_t, idx=idx: e.tensor_scalar(
                out=msum[:], in0=x_t, scalar1=mu_all[:, idx : idx + 1],
                scalar2=r1_all[:, idx : idx + 1], op0=ALU.subtract, op1=ALU.mult),
                reads=(bf("x2"), bf("mu_all"), bf("r1_all")), writes=(bf("msum"),))
            vop(lambda e, tt=tt: e.tensor_tensor(
                out=msum[:], in0=msum[:], in1=om_tm[:, tt, :], op=ALU.add),
                reads=(bf("msum"), bf("om_tm")), writes=(bf("msum"),))
            vop(lambda e: e.reduce_sum(out=colA[:], in_=msum[:], axis=AX.X),
                reads=(bf("msum"),), writes=(bf("colA"),))
            aop(lambda e: e.activation(out=sqjunk[:, : c.D], in_=msum[:],
                                       func=AF.Square, accum_out=colB[:]),
                reads=(bf("msum"),), writes=(bf("colB"), bf("sqjunk")))
            vop(lambda e: e.tensor_scalar(out=colA[:], in0=colA[:],
                                          scalar1=1.0 / c.D, scalar2=None,
                                          op0=ALU.mult),
                reads=(bf("colA"),), writes=(bf("colA"),))
            vop(lambda e: e.tensor_scalar(out=colC[:], in0=colB[:],
                                          scalar1=1.0 / c.D, scalar2=None,
                                          op0=ALU.mult),
                reads=(bf("colB"),), writes=(bf("colC"),))
            vop(lambda e: e.scalar_tensor_tensor(
                out=colD[:], in0=colA[:], scalar=colA[:], in1=colC[:],
                op0=ALU.mult, op1=ALU.subtract),
                reads=(bf("colA"), bf("colC")), writes=(bf("colD"),))
            vop(lambda e: e.tensor_scalar(out=colD[:], in0=colD[:], scalar1=-1.0,
                                          scalar2=None, op0=ALU.mult),
                reads=(bf("colD"),), writes=(bf("colD"),))
            aop(lambda e: e.activation(out=colC[:], in_=colD[:], func=AF.Sqrt,
                                       bias=eps_col[:]),
                reads=(bf("colD"), bf("eps")), writes=(bf("colC"),))
            vop(lambda e: e.reciprocal(out=colC[:], in_=colC[:]),
                reads=(bf("colC"),), writes=(bf("colC"),))
            vop(lambda e: e.tensor_scalar(
                out=msum[:], in0=msum[:], scalar1=colA[:], scalar2=colC[:],
                op0=ALU.subtract, op1=ALU.mult),
                reads=(bf("msum"), bf("colA"), bf("colC")), writes=(bf("msum"),))
            vop(lambda e, x_t=x_t, tt=tt: e.tensor_tensor(
                out=x1st[:, tt, :], in0=x_t, in1=msum[:], op=ALU.add),
                reads=(bf("x2"), bf("msum")), writes=(bf("x1st"),))
            vop(lambda e, tt=tt: e.reduce_sum(out=colA[:], in_=x1st[:, tt, :],
                                              axis=AX.X),
                reads=(bf("x1st"),), writes=(bf("colA"),))
            aop(lambda e, tt=tt: e.activation(out=sqjunk[:, : c.D],
                                              in_=x1st[:, tt, :], func=AF.Square,
                                              accum_out=colB[:]),
                reads=(bf("x1st"),), writes=(bf("colB"), bf("sqjunk")))
            vop(lambda e, idx=idx: e.tensor_scalar(
                out=muc_all[:, idx : idx + 1], in0=colA[:], scalar1=1.0 / c.D,
                scalar2=None, op0=ALU.mult),
                reads=(bf("colA"),), writes=(bf("muc_all"),))
            vop(lambda e: e.tensor_scalar(out=colC[:], in0=colB[:],
                                          scalar1=1.0 / c.D, scalar2=None,
                                          op0=ALU.mult),
                reads=(bf("colB"),), writes=(bf("colC"),))
            vop(lambda e, idx=idx: e.scalar_tensor_tensor(
                out=colD[:], in0=muc_all[:, idx : idx + 1],
                scalar=muc_all[:, idx : idx + 1], in1=colC[:],
                op0=ALU.mult, op1=ALU.subtract),
                reads=(bf("muc_all"), bf("colC")), writes=(bf("colD"),))
            vop(lambda e: e.tensor_scalar(out=colD[:], in0=colD[:], scalar1=-1.0,
                                          scalar2=None, op0=ALU.mult),
                reads=(bf("colD"),), writes=(bf("colD"),))
            aop(lambda e: e.activation(out=colC[:], in_=colD[:], func=AF.Sqrt,
                                       bias=eps_col[:]),
                reads=(bf("colD"), bf("eps")), writes=(bf("colC"),))
            vop(lambda e, idx=idx: e.reciprocal(out=rc_all[:, idx : idx + 1],
                                                in_=colC[:]),
                reads=(bf("colC"),), writes=(bf("rc_all"),))
        st(x1v[:, tok0 // 128 : tok0 // 128 + ntt2, :], x1st[:], "st_x1",
           bf("x1st"), wb=(bf("sc_x1"),))

    K.barrier()
    es_2b.close()

    es_3a = ExitStack()
    sw1_sb = sbuf(es_3a, "sw1_sb", (P, c.DKC, c.HS), BF16)
    sw3_sb = sbuf(es_3a, "sw3_sb", (P, c.DKC, c.HS), BF16)
    sw2_sb = sbuf(es_3a, "sw2_sb", (P, c.HS // 128, c.D), BF16)
    ntt3 = c.T3 // 128
    x13 = sbuf(es_3a, "x13", (P, ntt3, c.D))
    xn3_tm = sbuf(es_3a, "xn3_tm", (P, ntt3, c.D), BF16)
    xn3T = sbuf(es_3a, "xn3T", (P, c.DKC, c.T3), BF16)
    h1_sb = sbuf(es_3a, "h1_sb", (P, c.HS // 128, c.T3), BF16)
    h3_sb = sbuf(es_3a, "h3_sb", (P, c.HS // 128, c.T3), BF16)
    sh_sb = sbuf(es_3a, "sh_sb", (P, c.DKC, c.T3), BF16)
    sT = sbuf(es_3a, "sT", (P, c.E))
    selT = sbuf(es_3a, "selT", (P, c.E))
    mk1 = sbuf(es_3a, "mk1", (P, c.E))
    mk2 = sbuf(es_3a, "mk2", (P, c.E))
    tmp8 = sbuf(es_3a, "tmp8", (P, c.E))
    ctb = sbuf(es_3a, "ctb", (P, c.E), BF16)
    combst = sbuf(es_3a, "combst", (c.E, c.T3), BF16)

    ld(sw1_sb[:], sw1v[:], "ld_sw1", bf("sw1_sb"), rb=(bf("sc_sw1"),))
    ld(sw3_sb[:], sw3v[:], "ld_sw3", bf("sw3_sb"), rb=(bf("sc_sw3"),))
    ld(sw2_sb[:], sw2v[:], "ld_sw2", bf("sw2_sb"), rb=(bf("sc_sw2"),))

    nch3 = c.NT // c.T3
    for cc_ in range(nch3):
        tok0 = cc_ * c.T3
        ld(x13[:], x1v[:, tok0 // 128 : tok0 // 128 + ntt3, :], "ld_x1",
           bf("x13"), rb=(bf("sc_x1"),))
        for tt in range(ntt3):
            idx = tok0 // 128 + tt
            vop(lambda e, tt=tt, idx=idx: e.tensor_scalar(
                out=xn3_tm[:, tt, :], in0=x13[:, tt, :],
                scalar1=muc_all[:, idx : idx + 1], scalar2=rc_all[:, idx : idx + 1],
                op0=ALU.subtract, op1=ALU.mult),
                reads=(bf("x13"), bf("muc_all"), bf("rc_all")),
                writes=(bf("xn3_tm"),))
        for tt in range(ntt3):
            for dkc in range(c.DKC):
                i = (tt * c.DKC + dkc) % 2
                pt = ptr[i]
                pbuf = bf(f"ptr{i}")
                transp(pt[:], xn3_tm[:, tt, dkc * 128 : (dkc + 1) * 128],
                       (bf("xn3_tm"),), pbuf)
                vop(lambda e, pt=pt, dkc=dkc, tt=tt: e.tensor_copy(
                    xn3T[:, dkc, tt * 128 : (tt + 1) * 128], pt[:]),
                    reads=(pbuf,), writes=(bf("xn3T"),))
        st(xn3v[:, :, tok0 : tok0 + c.T3], xn3T[:], "st_xn3", bf("xn3T"),
           wb=(bf("sc_xn3"),))
        for hc in range(c.HS // 128):
            p = pm[hc % 2]
            pb = bf(f"pm{hc % 2}")
            pairs = [(sw1_sb[:, kc, hc * 128 : (hc + 1) * 128], xn3T[:, kc, :])
                     for kc in range(c.DKC)]
            mmacc(p[:, : c.T3], pairs, (bf("sw1_sb"), bf("xn3T")), pb)
            aop(lambda e, p=p, hc=hc: e.activation(
                out=h1_sb[:, hc, :], in_=p[:, : c.T3], func=AF.Silu),
                reads=(pb,), writes=(bf("h1_sb"),))
            p2_ = pm[2 + hc % 2]
            pb2 = bf(f"pm{2 + hc % 2}")
            pairs = [(sw3_sb[:, kc, hc * 128 : (hc + 1) * 128], xn3T[:, kc, :])
                     for kc in range(c.DKC)]
            mmacc(p2_[:, : c.T3], pairs, (bf("sw3_sb"), bf("xn3T")), pb2)
            aop(lambda e, p2_=p2_, hc=hc: e.activation(
                out=h3_sb[:, hc, :], in_=p2_[:, : c.T3], func=AF.Copy),
                reads=(pb2,), writes=(bf("h3_sb"),))
            vop(lambda e, hc=hc: e.tensor_tensor(
                out=h3_sb[:, hc, :], in0=h1_sb[:, hc, :], in1=h3_sb[:, hc, :],
                op=ALU.mult),
                reads=(bf("h1_sb"), bf("h3_sb")), writes=(bf("h3_sb"),))
        for oc in range(c.DKC):
            p = pm[oc % 2]
            pb = bf(f"pm{oc % 2}")
            pairs = [(sw2_sb[:, hc, oc * 128 : (oc + 1) * 128], h3_sb[:, hc, :])
                     for hc in range(c.HS // 128)]
            mmacc(p[:, : c.T3], pairs, (bf("sw2_sb"), bf("h3_sb")), pb)
            vop(lambda e, p=p, oc=oc: e.tensor_copy(sh_sb[:, oc, :], p[:, : c.T3]),
                reads=(pb,), writes=(bf("sh_sb"),))
        st(shv[:, :, tok0 : tok0 + c.T3], sh_sb[:], "st_sh", bf("sh_sb"),
           wb=(bf("sc_sh"),))
        for tt in range(ntt3):
            psr = psm if tt % 2 == 0 else py[0]
            psrb = bf("psm" if tt % 2 == 0 else "py0")
            pairs = [(xn3T[:, kc, tt * 128 : (tt + 1) * 128],
                      rw_bf[:, kc * c.E : (kc + 1) * c.E]) for kc in range(c.DKC)]
            mmacc(psr[:, : c.E], pairs, (bf("xn3T"), bf("rw")), psrb)
            aop(lambda e, psr=psr: e.activation(out=sT[:], in_=psr[:, : c.E],
                                                func=AF.Sigmoid),
                reads=(psrb,), writes=(bf("sT"),))
            vop(lambda e: e.tensor_tensor(out=selT[:], in0=sT[:], in1=biasbc[:],
                                          op=ALU.add),
                reads=(bf("sT"), bf("biasbc")), writes=(bf("selT"),))
            vop(lambda e: e.reduce_max(out=colA[:], in_=selT[:], axis=AX.X),
                reads=(bf("selT"),), writes=(bf("colA"),))
            vop(lambda e: e.tensor_scalar(out=mk1[:], in0=selT[:], scalar1=colA[:],
                                          scalar2=None, op0=ALU.is_equal),
                reads=(bf("selT"), bf("colA")), writes=(bf("mk1"),))
            vop(lambda e: e.scalar_tensor_tensor(
                out=tmp8[:], in0=mk1[:], scalar=-1e30, in1=selT[:],
                op0=ALU.mult, op1=ALU.add),
                reads=(bf("mk1"), bf("selT")), writes=(bf("tmp8"),))
            vop(lambda e: e.reduce_max(out=colB[:], in_=tmp8[:], axis=AX.X),
                reads=(bf("tmp8"),), writes=(bf("colB"),))
            vop(lambda e: e.tensor_scalar(out=mk2[:], in0=tmp8[:], scalar1=colB[:],
                                          scalar2=None, op0=ALU.is_equal),
                reads=(bf("tmp8"), bf("colB")), writes=(bf("mk2"),))
            vop(lambda e: e.tensor_tensor(out=tmp8[:], in0=mk1[:], in1=sT[:],
                                          op=ALU.mult),
                reads=(bf("mk1"), bf("sT")), writes=(bf("tmp8"),))
            vop(lambda e: e.reduce_sum(out=colA[:], in_=tmp8[:], axis=AX.X),
                reads=(bf("tmp8"),), writes=(bf("colA"),))
            vop(lambda e: e.tensor_tensor(out=tmp8[:], in0=mk2[:], in1=sT[:],
                                          op=ALU.mult),
                reads=(bf("mk2"), bf("sT")), writes=(bf("tmp8"),))
            vop(lambda e: e.reduce_sum(out=colB[:], in_=tmp8[:], axis=AX.X),
                reads=(bf("tmp8"),), writes=(bf("colB"),))
            vop(lambda e: e.tensor_tensor(out=colC[:], in0=colA[:], in1=colB[:],
                                          op=ALU.add),
                reads=(bf("colA"), bf("colB")), writes=(bf("colC"),))
            vop(lambda e: e.tensor_scalar(out=colD[:], in0=colC[:], scalar1=1e-9,
                                          scalar2=None, op0=ALU.add),
                reads=(bf("colC"),), writes=(bf("colD"),))
            vop(lambda e: e.reciprocal(out=colD[:], in_=colD[:]),
                reads=(bf("colD"),), writes=(bf("colD"),))
            vop(lambda e: e.tensor_tensor(out=colA[:], in0=colA[:], in1=colD[:],
                                          op=ALU.mult),
                reads=(bf("colA"), bf("colD")), writes=(bf("colA"),))
            vop(lambda e: e.tensor_tensor(out=colB[:], in0=colB[:], in1=colD[:],
                                          op=ALU.mult),
                reads=(bf("colB"), bf("colD")), writes=(bf("colB"),))
            vop(lambda e: e.tensor_scalar(out=colE[:], in0=colC[:], scalar1=1e-9,
                                          scalar2=None, op0=ALU.is_gt),
                reads=(bf("colC"),), writes=(bf("colE"),))
            vop(lambda e: e.tensor_scalar(out=colF[:], in0=colA[:], scalar1=0.5,
                                          scalar2=None, op0=ALU.subtract),
                reads=(bf("colA"),), writes=(bf("colF"),))
            vop(lambda e: e.scalar_tensor_tensor(
                out=colF[:], in0=colF[:], scalar=0.5, in1=colE[:],
                op0=ALU.bypass, op1=ALU.mult),
                reads=(bf("colF"), bf("colE")), writes=(bf("colF"),))
            vop(lambda e: e.tensor_scalar(out=colF[:], in0=colF[:], scalar1=0.5,
                                          scalar2=None, op0=ALU.add),
                reads=(bf("colF"),), writes=(bf("colF"),))
            vop(lambda e: e.tensor_scalar(out=colG[:], in0=colB[:], scalar1=0.5,
                                          scalar2=None, op0=ALU.subtract),
                reads=(bf("colB"),), writes=(bf("colG"),))
            vop(lambda e: e.scalar_tensor_tensor(
                out=colG[:], in0=colG[:], scalar=0.5, in1=colE[:],
                op0=ALU.bypass, op1=ALU.mult),
                reads=(bf("colG"), bf("colE")), writes=(bf("colG"),))
            vop(lambda e: e.tensor_scalar(out=colG[:], in0=colG[:], scalar1=0.5,
                                          scalar2=None, op0=ALU.add),
                reads=(bf("colG"),), writes=(bf("colG"),))
            vop(lambda e: e.tensor_scalar(out=tmp8[:], in0=mk1[:], scalar1=colF[:],
                                          scalar2=None, op0=ALU.mult),
                reads=(bf("mk1"), bf("colF")), writes=(bf("tmp8"),))
            vop(lambda e: e.scalar_tensor_tensor(
                out=ctb[:], in0=mk2[:], scalar=colG[:], in1=tmp8[:],
                op0=ALU.mult, op1=ALU.add),
                reads=(bf("mk2"), bf("colG"), bf("tmp8")), writes=(bf("ctb"),))
            i = tt % 2
            pt = ptr[i]
            pbuf = bf(f"ptr{i}")
            transp(pt[: c.E, :], ctb[:], (bf("ctb"),), pbuf)
            vop(lambda e, pt=pt, tt=tt: e.tensor_copy(
                combst[:, tt * 128 : (tt + 1) * 128], pt[: c.E, :]),
                reads=(pbuf,), writes=(bf("combst"),))
        st(sc_comb[:, tok0 : tok0 + c.T3], combst[:], "st_comb", bf("combst"),
           wb=(bf("sc_comb"),))

    K.barrier()
    es_3a.close()

    es_3b = ExitStack()
    ntt3b = c.T3B // 128
    xn3b = sbuf(es_3b, "xn3b", (P, c.DKC, c.T3B), BF16)
    shb = sbuf(es_3b, "shb", (P, c.DKC, c.T3B), BF16)
    x1b = sbuf(es_3b, "x1b", (P, ntt3b, c.D))
    ew1_sb = [sbuf(es_3b, f"ew1_sb{i}", (P, c.DKC, c.HR), BF16) for i in range(2)]
    ew2_sb = [sbuf(es_3b, f"ew2_sb{i}", (P, c.HR // 128, c.D), BF16)
              for i in range(2)]
    combb = sbuf(es_3b, "combb", (P, c.T3B), BF16)
    combch = sbuf(es_3b, "combch", (c.E, c.T3B), BF16)
    he_sb = sbuf(es_3b, "he_sb", (P, c.HR // 128, c.T3B), BF16)
    racc = sbuf(es_3b, "racc", (P, c.DKC, c.T3B))
    ffn_tm = sbuf(es_3b, "ffn_tm", (P, ntt3b, c.D), BF16)
    outst = sbuf(es_3b, "outst", (P, ntt3b, c.D))

    nch3b = c.NT // c.T3B
    for cc_ in range(nch3b):
        tok0 = cc_ * c.T3B
        b = tok0 // c.L
        tl0 = (tok0 % c.L) // 128
        ld(xn3b[:], xn3v[:, :, tok0 : tok0 + c.T3B], "ld_xn3b", bf("xn3b"),
           rb=(bf("sc_xn3"),))
        ld(shb[:], shv[:, :, tok0 : tok0 + c.T3B], "ld_shb", bf("shb"),
           rb=(bf("sc_sh"),))
        ld(x1b[:], x1v[:, tok0 // 128 : tok0 // 128 + ntt3b, :], "ld_x1b",
           bf("x1b"), rb=(bf("sc_x1"),))
        ld(combch[:], sc_comb[:, tok0 : tok0 + c.T3B], "ld_comb", bf("combch"),
           rb=(bf("sc_comb"),))
        for e_ in range(c.E):
            par = e_ % 2
            ld(ew1_sb[par][:], ew1v[e_], f"ld_ew1{par}",
               bf(f"ew1_{par}"), rb=(bf("sc_ew1"),))
            K.op(pool, lambda e, par=par, e_=e_: e.dma_start(
                out=ew2_sb[par][:], in_=ew2v[e_]),
                reads=(bf("sc_ew2"),), writes=(bf(f"ew2_{par}"),),
                chan=ch(f"ld_ew2{par}"))
            p = pm[3]
            pb = bf("pm3")
            K.op(pe, lambda e, p=p, e_=e_: e.matmul(
                p[:, : c.T3B], sele[:, e_ * 128 : (e_ + 1) * 128],
                combch[:], start=True, stop=True),
                reads=(bf("sele"), bf("combch")), writes=(pb,))
            vop(lambda e, p=p: e.tensor_copy(combb[:], p[:, : c.T3B]),
                reads=(pb,), writes=(bf("combb"),))
            for hc in range(c.HR // 128):
                p = pm[hc % 2]
                pbh = bf(f"pm{hc % 2}")
                pairs = [(ew1_sb[par][:, kc, hc * 128 : (hc + 1) * 128],
                          xn3b[:, kc, :]) for kc in range(c.DKC)]
                mmacc(p[:, : c.T3B], pairs, (bf(f"ew1_{par}"), bf("xn3b")), pbh)
                aop(lambda e, p=p, hc=hc: e.activation(
                    out=he_sb[:, hc, :], in_=p[:, : c.T3B], func=AF.Gelu),
                    reads=(pbh,), writes=(bf("he_sb"),))
                vop(lambda e, hc=hc: e.tensor_tensor(
                    out=he_sb[:, hc, :], in0=he_sb[:, hc, :], in1=combb[:],
                    op=ALU.mult),
                    reads=(bf("he_sb"), bf("combb")), writes=(bf("he_sb"),))
            for oc in range(c.DKC):
                p = pm[2 + oc % 2]
                pbo = bf(f"pm{2 + oc % 2}")
                pairs = [(ew2_sb[par][:, hc, oc * 128 : (oc + 1) * 128],
                          he_sb[:, hc, :]) for hc in range(c.HR // 128)]
                mmacc(p[:, : c.T3B], pairs, (bf(f"ew2_{par}"), bf("he_sb")), pbo)
                if e_ == 0:
                    vop(lambda e, p=p, oc=oc: e.tensor_copy(racc[:, oc, :],
                                                            p[:, : c.T3B]),
                        reads=(pbo,), writes=(bf("racc"),))
                else:
                    vop(lambda e, p=p, oc=oc: e.tensor_tensor(
                        out=racc[:, oc, :], in0=racc[:, oc, :], in1=p[:, : c.T3B],
                        op=ALU.add),
                        reads=(pbo, bf("racc")), writes=(bf("racc"),))
        for oc in range(c.DKC):
            vop(lambda e, oc=oc: e.tensor_tensor(
                out=shb[:, oc, :], in0=racc[:, oc, :], in1=shb[:, oc, :],
                op=ALU.add),
                reads=(bf("racc"), bf("shb")), writes=(bf("shb"),))
        for oc in range(c.DKC):
            for tt in range(ntt3b):
                i = (oc * ntt3b + tt) % 2
                pt = ptr[i]
                pbuf = bf(f"ptr{i}")
                transp(pt[:], shb[:, oc, tt * 128 : (tt + 1) * 128],
                       (bf("shb"),), pbuf)
                vop(lambda e, pt=pt, tt=tt, oc=oc: e.tensor_copy(
                    ffn_tm[:, tt, oc * 128 : (oc + 1) * 128], pt[:]),
                    reads=(pbuf,), writes=(bf("ffn_tm"),))
        for tt in range(ntt3b):
            f_t = ffn_tm[:, tt, :]
            vop(lambda e, f_t=f_t: e.reduce_sum(out=colA[:], in_=f_t, axis=AX.X),
                reads=(bf("ffn_tm"),), writes=(bf("colA"),))
            aop(lambda e, f_t=f_t: e.activation(out=sqjunk[:, : c.D], in_=f_t,
                                                func=AF.Square, accum_out=colB[:]),
                reads=(bf("ffn_tm"),), writes=(bf("colB"), bf("sqjunk")))
            vop(lambda e: e.tensor_scalar(out=colA[:], in0=colA[:],
                                          scalar1=1.0 / c.D, scalar2=None,
                                          op0=ALU.mult),
                reads=(bf("colA"),), writes=(bf("colA"),))
            vop(lambda e: e.tensor_scalar(out=colC[:], in0=colB[:],
                                          scalar1=1.0 / c.D, scalar2=None,
                                          op0=ALU.mult),
                reads=(bf("colB"),), writes=(bf("colC"),))
            vop(lambda e: e.scalar_tensor_tensor(
                out=colD[:], in0=colA[:], scalar=colA[:], in1=colC[:],
                op0=ALU.mult, op1=ALU.subtract),
                reads=(bf("colA"), bf("colC")), writes=(bf("colD"),))
            vop(lambda e: e.tensor_scalar(out=colD[:], in0=colD[:], scalar1=-1.0,
                                          scalar2=None, op0=ALU.mult),
                reads=(bf("colD"),), writes=(bf("colD"),))
            aop(lambda e: e.activation(out=colC[:], in_=colD[:], func=AF.Sqrt,
                                       bias=eps_col[:]),
                reads=(bf("colD"), bf("eps")), writes=(bf("colC"),))
            vop(lambda e: e.reciprocal(out=colC[:], in_=colC[:]),
                reads=(bf("colC"),), writes=(bf("colC"),))
            vop(lambda e, f_t=f_t, tt=tt: e.tensor_scalar(
                out=outst[:, tt, :], in0=f_t, scalar1=colA[:], scalar2=colC[:],
                op0=ALU.subtract, op1=ALU.mult),
                reads=(bf("ffn_tm"), bf("colA"), bf("colC")), writes=(bf("outst"),))
            vop(lambda e, tt=tt: e.tensor_tensor(
                out=outst[:, tt, :], in0=outst[:, tt, :], in1=x1b[:, tt, :],
                op=ALU.add),
                reads=(bf("outst"), bf("x1b")), writes=(bf("outst"),))
        st(ov[b, :, tl0 : tl0 + ntt3b, :], outst[:], "st_out", bf("outst"))

    K.barrier()
    es_3b.close()

    with nc.Block() as block:
        K.build(nc, block, es_g)
    es_g.close()
    return nc


def _host_layout(inputs, cfg):
    c = cfg
    P = 128
    f = np.ascontiguousarray
    d = {}
    d["W_in"] = inputs["L0_attn_W_in"]
    d["conv_w_pt"] = f(inputs["L0_attn_conv_w"].reshape(c.DINKC, P, c.CONV)
                       .transpose(1, 0, 2).reshape(P, c.DINKC * c.CONV))
    d["conv_b_pt"] = f(inputs["L0_attn_conv_b"].reshape(c.DINKC, P).T)
    d["W_xproj"] = inputs["L0_attn_W_xproj"]
    d["W_dt"] = inputs["L0_attn_W_dt"]
    d["b_dt_pt"] = f(inputs["L0_attn_b_dt"].reshape(c.DINKC, P).T)
    d["A_log_pt"] = f(inputs["L0_attn_A_log"].reshape(c.DINKC, P, c.N)
                      .transpose(1, 0, 2).reshape(P, c.DINKC * c.N))
    d["D_pt"] = f(inputs["L0_attn_D"].reshape(c.DINKC, P).T)
    d["W_out"] = inputs["L0_attn_W_out"]
    d["ln_w_pt"] = f(inputs["L0_attn_ln_w"].reshape(c.DKC, P).T)
    d["ln_b_pt"] = f(inputs["L0_attn_ln_b"].reshape(c.DKC, P).T)
    d["router_w_pt"] = f(inputs["L0_ffn_router_w"].T.reshape(c.DKC, P, c.E)
                         .transpose(1, 0, 2).reshape(P, c.DKC * c.E))
    d["router_bias_bc"] = f(np.broadcast_to(inputs["L0_ffn_router_bias"],
                                            (P, c.E)))
    d["sw1"] = inputs["L0_ffn_sw1"]
    d["sw3"] = inputs["L0_ffn_sw3"]
    d["sw2"] = inputs["L0_ffn_sw2"]
    d["ew1"] = inputs["L0_ffn_ew1"]
    d["ew2"] = inputs["L0_ffn_ew2"]
    import ml_dtypes
    d["ident_bf"] = np.eye(P, dtype=ml_dtypes.bfloat16)
    seln = np.zeros((c.N, c.N * P), dtype=ml_dtypes.bfloat16)
    for n in range(c.N):
        seln[n, n * P : (n + 1) * P] = 1.0
    d["sel_n"] = seln
    sele = np.zeros((c.E, c.E * P), dtype=ml_dtypes.bfloat16)
    for e in range(c.E):
        sele[e, e * P : (e + 1) * P] = 1.0
    d["sel_e"] = sele
    return d


def kernel(**inputs):
    cfg = FULL
    x = np.ascontiguousarray(inputs["L0_attn_x"])
    n_cores = 8
    shards = np.split(x, n_cores, axis=0)
    shared = _host_layout(inputs, cfg)
    nc = bass.Bass("TRN2", target_bir_lowering=False)
    build_core(nc, cfg)
    nc.finalize()
    in_maps = []
    for i in range(n_cores):
        m = dict(shared)
        m["x"] = shards[i]
        in_maps.append(m)
    res = run_bass_kernel_spmd(nc, in_maps, core_ids=list(range(n_cores)))
    return np.concatenate([r["out"] for r in res.results], axis=0)
